# version 11 (final)
# baseline (speedup 1.0000x reference)
import sys
if "/opt/trn_rl_repo" not in sys.path:
    sys.path.insert(0, "/opt/trn_rl_repo")
import numpy as np
from contextlib import ExitStack

import concourse.bass as bass
import concourse.mybir as mybir

F32 = mybir.dt.float32
BF16 = mybir.dt.bfloat16
I16 = mybir.dt.int16
I32 = mybir.dt.int32
U8 = mybir.dt.uint8
U16 = mybir.dt.uint16
AF = mybir.ActivationFunctionType
ALU = mybir.AluOpType
AX = mybir.AxisListType

P = 128
D = 2048
DT = D // P
TLOC = 2048
HALO = 128
TT = TLOC + HALO
TTILES = TT // P
NB = TLOC // P
H = 32
DK = 64
HG = 4
G = H // HG
DPG = HG * DK
NF = 16384
EXP_OFF = -20.0
SLAB = 1024
ACCOFF = DT * SLAB

TCH = [(0, 512), (512, 512), (1024, 512), (1536, 512), (2048, 128)]


class Prog:
    def __init__(self, nc, ctx):
        self.nc = nc
        self.ctx = ctx
        self.ops = []
        self.sems = {}
        self.cnt = {}

    def sem(self, name):
        if name not in self.sems:
            self.sems[name] = self.ctx.enter_context(self.nc.semaphore(name))
            self.cnt[name] = 0
        return self.sems[name]

    def emit(self, engine, fn, waits=(), inc=None, inc_by=1, tag=""):
        w = []
        for t in waits:
            if t is None:
                continue
            s, v, peng = t
            if peng == engine and not s.startswith("dma"):
                continue
            w.append((s, v))
        wd = {}
        for s, v in w:
            wd[s] = max(wd.get(s, 0), v)
        tok = None
        if inc is not None:
            self.sem(inc)
            self.cnt[inc] += inc_by
            tok = (inc, self.cnt[inc], engine if inc_by == 1 else "dma")
        self.ops.append((engine, fn, sorted(wd.items()), inc, inc_by, tag))
        return tok

    def marker(self, engine="vector"):
        return self.emit(engine, lambda e: e.engine_nop(), inc="mark", tag="marker")

    def dma(self, queue, out, in_, waits=(), tag="", stream=None):
        sname = "dma_" + (stream or queue)
        self.sem(sname)
        w = list(waits)
        if self.cnt[sname]:
            w.append((sname, self.cnt[sname], "dma"))

        def fn(eng):
            return eng.dma_start(out=out, in_=in_)
        return self.emit(queue, fn, w, inc=sname, inc_by=16, tag=tag)

    def build(self, block):
        sems = self.sems

        def make(engine_name):
            def closure(eng):
                for (e, fn, waits, inc, inc_by, tag) in self.ops:
                    if e != engine_name:
                        continue
                    for (s, v) in waits:
                        eng.wait_ge(sems[s], v)
                    inst = fn(eng)
                    if inc is not None:
                        assert inst is not None, f"op {tag} returned no instruction"
                        inst.then_inc(sems[inc], inc_by)
            return closure

        block.sync(make("sync"))
        block.scalar(make("scalar"))
        block.vector(make("vector"))
        block.tensor(make("tensor"))
        block.gpsimd(make("gpsimd"))


def F(tok):
    return None if tok is None else (tok[0], tok[1], "__force__")


class Banks:
    def __init__(self, n=8):
        self.tok = [None] * n

    def gate(self, b):
        return self.tok[b]

    def done(self, b, token):
        self.tok[b] = token


def build_program(phases=("W", "A", "B"), debug=()):
    nc = bass.Bass("TRN2", target_bir_lowering=False, debug=False)
    ctx = ExitStack()

    d_in = {}
    d_in["x_loc"] = nc.dram_tensor("x_loc", [TLOC, D], F32, kind="ExternalInput").ap()
    d_in["x_halo"] = nc.dram_tensor("x_halo", [HALO, D], F32, kind="ExternalInput").ap()
    for w in ("wq", "wk", "wv", "wo"):
        d_in[w] = nc.dram_tensor(w, [D, D], F32, kind="ExternalInput").ap()
    d_in["gate_p"] = nc.dram_tensor("gate_p", [NF, D // 2], U8, kind="ExternalInput").ap()
    d_in["gate_s"] = nc.dram_tensor("gate_s", [NF, D // 32], F32, kind="ExternalInput").ap()
    d_in["up_p"] = nc.dram_tensor("up_p", [NF, D // 2], U8, kind="ExternalInput").ap()
    d_in["up_s"] = nc.dram_tensor("up_s", [NF, D // 32], F32, kind="ExternalInput").ap()
    d_in["down_p"] = nc.dram_tensor("down_p", [D, NF // 2], U8, kind="ExternalInput").ap()
    d_in["down_s"] = nc.dram_tensor("down_s", [D, NF // 32], F32, kind="ExternalInput").ap()
    d_in["maskT_first"] = nc.dram_tensor("maskT_first", [2 * P, P], BF16, kind="ExternalInput").ap()
    d_in["maskT_rest"] = nc.dram_tensor("maskT_rest", [2 * P, P], BF16, kind="ExternalInput").ap()

    out_t = nc.dram_tensor("out", [TLOC, D], F32, kind="ExternalOutput").ap()

    scr_g = nc.dram_tensor("scr_g", [D, NF], BF16, kind="Internal").ap()
    scr_u = nc.dram_tensor("scr_u", [D, NF], BF16, kind="Internal").ap()
    scr_d = nc.dram_tensor("scr_d", [NF, D], BF16, kind="Internal").ap()

    dbg = {}

    def dbg_out(name, shape, dt=F32):
        if name in debug:
            dbg[name] = nc.dram_tensor("dbg_" + name, shape, dt, kind="ExternalOutput").ap()
            return dbg[name]
        return None

    AR1 = ctx.enter_context(nc.sbuf_tensor("AR1", [P, 34816], BF16))
    AR2 = ctx.enter_context(nc.sbuf_tensor("AR2", [P, 32768], BF16))
    AR3 = ctx.enter_context(nc.sbuf_tensor("AR3", [P, 32768], BF16))

    def v1(a, b):
        return AR1[:, a:b]

    def v2(a, b):
        return AR2[:, a:b]

    def v3(a, b):
        return AR3[:, a:b]

    xnT = v1(0, 34816)
    x2 = v1(0, 32768)
    otile = v1(32768, 34816).bitcast(F32)
    aoT = v2(0, 32768)
    W_TAIL = 22912
    d_in8 = v2(22912, 23936).bitcast(U8)
    d_b16 = v2(23936, 24960).bitcast(U16)
    d_t1 = v2(24960, 25984).bitcast(U16)
    d_t2 = v2(25984, 27008).bitcast(U16)
    d_t1h = v2(27008, 28032).bitcast(U16)
    d_t2h = v2(28032, 29056).bitcast(U16)
    d_wn = v2(29056, 31104)
    d_sc = v2(31104, 31360).bitcast(F32)
    d_wt = v2(31360, 32384)
    d_scb = v2(32384, 32512)
    d_sce = v1(32768, 34816)
    xtile = v2(0, 8192).bitcast(F32)
    qT = v3(0, 4352)
    kT = v3(4352, 8704)
    vN = v3(8704, 13056)
    wrow = v3(13056, 21248).bitcast(F32)
    wrow_b = v3(21248, 25344)
    wT_a = v3(25344, 27392)
    wT_v = v3(27392, 31488)
    d_wt_hi = v3(31488, 32512)
    xn2T = v3(0, 16384)
    wgt = v3(16384, 20480)
    wut = v3(20480, 24576)
    wdt = v3(24576, 28672)
    xnnat = v3(28672, 30720)
    g1 = v3(30720, 31232)

    def gsb(name, pdim, fsize, dt):
        return ctx.enter_context(nc.sbuf_tensor(name, [pdim, fsize], dt))

    stats = gsb("stats", P, 16, F32)
    eps_t = gsb("eps_t", P, 1, F32)
    c20 = gsb("c20", P, 1, F32)
    ident = gsb("ident", P, P, BF16)
    identf = gsb("identf", P, P, F32)
    ones_col = gsb("ones_col", P, 1, BF16)
    ones_r64 = gsb("ones_r64", 1, DK, F32)
    maskTs = gsb("maskTs", P, 2 * 2 * P, BF16)
    expb = gsb("expb", P, 2 * 2 * P, BF16)
    recip = gsb("recip", 1, HG * P, F32)
    rbc = gsb("rbc", P, 2 * P, F32)
    projb = gsb("projb", P, 512, BF16)
    xrld = gsb("xrld", P, 4 * P, F32)

    psA = [ctx.enter_context(nc.psum_tensor(f"psA{i}", [P, 512], F32)) for i in range(8)]

    pr = Prog(nc, ctx)
    bk = Banks()

    t00 = pr.emit("gpsimd", lambda g: g.memset(identf[:], 0.0), inc="g")
    t0 = pr.emit("gpsimd", lambda g: g.affine_select(
        out=identf[:], in_=identf[:], compare_op=ALU.not_equal, fill=1.0, base=0,
        pattern=[[-1, P]], channel_multiplier=1), waits=[F(t00)], inc="g")
    T_ID = pr.emit("vector", lambda v: v.tensor_copy(ident[:], identf[:]), waits=[t0], inc="v")
    pr.emit("vector", lambda v: v.memset(ones_col[:], 1.0), inc="v")
    pr.emit("vector", lambda v: v.memset(ones_r64[:], 1.0), inc="v")
    pr.emit("vector", lambda v: v.memset(eps_t[:], 1e-5), inc="v")
    T_C20 = pr.emit("vector", lambda v: v.memset(c20[:], EXP_OFF), inc="v")
    T_MSK0 = pr.dma("sync", maskTs[:, 0:2 * P],
                    d_in["maskT_first"].rearrange("(a p) b -> p a b", p=P), tag="mskf")
    T_MSK1 = pr.dma("sync", maskTs[:, 2 * P:4 * P],
                    d_in["maskT_rest"].rearrange("(a p) b -> p a b", p=P), tag="mskr")

    W_UNITS = []
    if "W" in phases:
        for mat, sp, ss, n_rt, n_cc, scr in (
                ("g", d_in["gate_p"], d_in["gate_s"], NF // P, 1, scr_g),
                ("u", d_in["up_p"], d_in["up_s"], NF // P, 1, scr_u),
                ("d", d_in["down_p"], d_in["down_s"], D // P, 8, scr_d)):
            for rt in range(n_rt):
                for cc in range(n_cc):
                    W_UNITS.append((mat, sp, ss, scr, rt, cc))

    wst = {"ui": 0, "tok_in": [None, None], "tok_widen": [None, None],
           "tok_sc": [None, None], "tok_sm": [None, None], "tok_b16": None,
           "tok_scb": [None, None],
           "tok_wn": [None, None], "tok_store": [None, None],
           "last_store": None, "last_mm": None, "last_sm": None}

    def emit_w_unit():
        ui = wst["ui"]
        if ui >= len(W_UNITS):
            return False
        wst["ui"] += 1
        mat, sp, ss, scr, rt, cc = W_UNITS[ui]
        buf = ui % 2
        inb = d_in8[:, buf * 1024:(buf + 1) * 1024]
        scb = d_sc[:, buf * 64:(buf + 1) * 64]
        rows = slice(rt * P, (rt + 1) * P)
        t_in = pr.dma("sync", inb, sp[rows, cc * 1024:(cc + 1) * 1024],
                      waits=[wst["tok_widen"][buf]], tag=f"Win{ui}", stream=f"wpk{buf}")
        t_sc = pr.dma("sync", scb, ss[rows, cc * 64:(cc + 1) * 64],
                      waits=[wst["tok_scb"][buf]], tag=f"Wsc{ui}", stream=f"wsc{buf}")
        t_wd = pr.emit("vector", lambda v, inb=inb: v.tensor_copy(d_b16[:], inb),
                       waits=[t_in, wst["tok_b16"]], inc="v", tag=f"Wwd{ui}")
        wst["tok_widen"][buf] = t_wd
        scbb = d_scb[:, buf * 64:(buf + 1) * 64]
        scee = d_sce[:, buf * 1024:(buf + 1) * 1024]
        t_scb = pr.emit("scalar", lambda s, scbb=scbb, scb=scb: s.copy(scbb, scb),
                        waits=[t_sc, wst["tok_scb"][buf]], inc="s", tag=f"Wscb{ui}")
        t_sce = pr.emit("scalar", lambda s, scee=scee, scbb=scbb: s.copy(
            scee.rearrange("p (b r) -> p b r", r=16),
            scbb.unsqueeze(2).broadcast_to((P, 64, 16))),
            waits=[F(t_scb), wst["tok_sm"][buf]], inc="s", tag=f"Wsce{ui}")
        wst["tok_scb"][buf] = t_sce
        pr.emit("vector", lambda v: v.tensor_scalar(
            d_t1[:], d_b16[:], 7, 6, ALU.bitwise_and, ALU.logical_shift_left),
            waits=[t_wd], inc="v", tag=f"Wc{ui}")
        pr.emit("vector", lambda v: v.tensor_scalar(
            d_t1h[:], d_b16[:], 112, 2, ALU.bitwise_and, ALU.logical_shift_left), inc="v")
        pr.emit("vector", lambda v: v.tensor_scalar(
            d_t2[:], d_b16[:], 8, 12, ALU.bitwise_and, ALU.logical_shift_left), inc="v")
        t_b16free = pr.emit("vector", lambda v: v.tensor_scalar(
            d_t2h[:], d_b16[:], 128, 8, ALU.bitwise_and, ALU.logical_shift_left), inc="v")
        wst["tok_b16"] = t_b16free
        pr.emit("vector", lambda v: v.tensor_tensor(
            d_t2[:], d_t1[:], d_t2[:], ALU.add), inc="v")
        pr.emit("vector", lambda v: v.tensor_tensor(
            d_t2h[:], d_t1h[:], d_t2h[:], ALU.add), inc="v")
        pr.emit("vector", lambda v: v.tensor_single_scalar(
            d_t1[:], d_t1[:], 6, ALU.logical_shift_right), inc="v")
        pr.emit("vector", lambda v: v.tensor_single_scalar(
            d_t1h[:], d_t1h[:], 6, ALU.logical_shift_right), inc="v")
        pr.emit("vector", lambda v: v.tensor_scalar(
            d_t1[:], d_t1[:], 251, 252, ALU.mult, ALU.min), inc="v")
        pr.emit("vector", lambda v: v.tensor_scalar(
            d_t1h[:], d_t1h[:], 251, 252, ALU.mult, ALU.min), inc="v")
        pr.emit("vector", lambda v: v.tensor_single_scalar(
            d_t1[:], d_t1[:], 6, ALU.logical_shift_left), inc="v")
        pr.emit("vector", lambda v: v.tensor_single_scalar(
            d_t1h[:], d_t1h[:], 6, ALU.logical_shift_left), inc="v")
        t_el = pr.emit("vector", lambda v: v.tensor_tensor(
            d_t2[:], d_t1[:], d_t2[:], ALU.add), inc="v")
        t_eh = pr.emit("vector", lambda v: v.tensor_tensor(
            d_t2h[:], d_t1h[:], d_t2h[:], ALU.add), inc="v")
        sm_toks = []
        for koff, tsrc, t_e in ((0, d_t2, t_el), (1, d_t2h, t_eh)):
            patf = tsrc[:, :].bitcast(BF16)
            dst = d_wn[:, koff * 1024:(koff + 1) * 1024]
            t_sm = pr.emit("vector", lambda v, dst=dst, patf=patf, scee=scee:
                           v.tensor_tensor(dst, patf, scee, ALU.mult),
                           waits=[t_e, t_sce, wst["tok_wn"][koff]], inc="v",
                           tag=f"Wsm{ui}_{koff}")
            sm_toks.append(t_sm)
        wst["tok_sm"][buf] = sm_toks[1]
        wst["last_sm"] = sm_toks[1]
        if mat == "d":
            rr = scr.rearrange("(c i p two) d -> p c i two d", c=8, i=8, p=P, two=2)
        else:
            rr = scr.rearrange("(i p two) n -> p i two n", i=8, p=P, two=2)
        for koff in (0, 1):
            stg = d_wt if koff == 0 else d_wt_hi
            ev_toks = []
            last_mm = None
            for bg in range(2):
                b = 2 * koff + bg
                for j4 in range(4):
                    jj = koff * 8 + bg * 4 + j4
                    t_mm = pr.emit("tensor", lambda t, jj=jj, b=b, j4=j4: t.matmul(
                        psA[b][:, j4 * P:(j4 + 1) * P], d_wn[:, jj * P:(jj + 1) * P],
                        ident[:], start=True, stop=True),
                        waits=[sm_toks[koff], T_ID] + ([bk.gate(b)] if j4 == 0 else []),
                        inc="pe", tag=f"WT{ui}_{jj}")
                    last_mm = t_mm
                t_ev = pr.emit("scalar", lambda s, bg=bg, b=b, stg=stg: s.copy(
                    stg[:, bg * 512:(bg + 1) * 512], psA[b][:, 0:512]),
                    waits=[t_mm, wst["tok_store"][koff] if bg == 0 else None], inc="s",
                    tag=f"WTe{ui}_{koff}_{bg}")
                bk.done(b, t_ev)
                ev_toks.append(t_ev)
            wst["tok_wn"][koff] = last_mm
            if mat == "d":
                dst3 = rr[:, cc, :, koff, rt * P:(rt + 1) * P]
            else:
                dst3 = rr[:, :, koff, rt * P:(rt + 1) * P]
            t_st = pr.dma("scalar", dst3, stg[:, :].rearrange("p (i c) -> p i c", i=8),
                          waits=[F(t) for t in ev_toks], tag=f"Wst{ui}_{koff}",
                          stream=f"wst{koff}")
            wst["tok_store"][koff] = t_st
            wst["last_store"] = t_st
        wst["last_mm"] = wst["tok_wn"][1]
        return True

    def pump(n):
        for _ in range(n):
            if not emit_w_unit():
                return

    def drain_w():
        while emit_w_unit():
            pass

    if "A" not in phases:
        drain_w()
    T_W_V = None
    T_W_REL = []

    for nm in ("scrg", "scrd"):
        ap = dbg_out(nm, [D, 512] if nm == "scrg" else [512, D], BF16)
        if ap is not None:
            drain_w()
            pr.dma("sync", ap, scr_g[:, 0:512] if nm == "scrg" else scr_d[0:512, :],
                   waits=[wst["last_store"]], tag="dbg" + nm)
    ap = dbg_out("scrd0", [NF, P], BF16)
    if ap is not None:
        drain_w()
        pr.dma("sync", ap, scr_d[:, 0:P], waits=[wst["last_store"]], tag="dbgscrd0")

    def emit_ln(x_ap, out_ap, waits, tag):
        t_sum = pr.emit("vector", lambda v: v.reduce_sum(stats[:, 0:1], x_ap, axis=AX.X),
                        waits=waits, inc="v", tag=tag + "sum")
        t_sq = pr.emit("scalar", lambda s: s.activation(xnnat[:], x_ap, AF.Square,
                                                        accum_out=stats[:, 1:2]),
                       waits=waits, inc="s", tag=tag + "sq")
        t_mean = pr.emit("vector", lambda v: v.tensor_scalar(
            stats[:, 2:3], stats[:, 0:1], 1.0 / D, None, ALU.mult), waits=[F(t_sum)], inc="v")
        t_e2 = pr.emit("vector", lambda v: v.tensor_scalar(
            stats[:, 3:4], stats[:, 1:2], 1.0 / D, None, ALU.mult), waits=[t_sq], inc="v")
        t_var = pr.emit("vector", lambda v: v.scalar_tensor_tensor(
            stats[:, 4:5], stats[:, 2:3], stats[:, 2:3], stats[:, 3:4], ALU.mult, ALU.subtract),
            waits=[F(t_mean), F(t_e2)], inc="v", tag=tag + "var")
        t_sd = pr.emit("scalar", lambda s: s.activation(stats[:, 5:6], stats[:, 4:5], AF.Sqrt,
                                                        bias=eps_t[:], scale=-1.0),
                       waits=[t_var], inc="s")
        t_rstd = pr.emit("vector", lambda v: v.reciprocal(stats[:, 6:7], stats[:, 5:6]),
                         waits=[t_sd], inc="v")
        t_b = pr.emit("vector", lambda v: v.tensor_scalar(
            stats[:, 7:8], stats[:, 2:3], stats[:, 6:7], -1.0, ALU.mult, ALU.mult),
            waits=[F(t_rstd)], inc="v")
        t_xn = pr.emit("vector", lambda v: v.tensor_scalar(
            out_ap, x_ap, stats[:, 6:7], stats[:, 7:8], ALU.mult, ALU.add),
            waits=[F(t_b), t_sq], inc="v", tag=tag + "xn")
        return t_xn, t_sd

    tok_stats = None
    T_X2 = None
    T_AOT = None
    if "A" in phases:
        drain_w()
        tok_xtile = [None, None]
        tok_xnnat = None
        for tt in range(TTILES):
            buf = tt % 2
            xt = xtile[:, buf * D:(buf + 1) * D]
            src = d_in["x_halo"] if tt == 0 else d_in["x_loc"][(tt - 1) * P:tt * P, :]
            t_ld = pr.dma("sync", xt, src, waits=[tok_xtile[buf], T_W_V], tag=f"A1ld{tt}", stream=f"a1x{buf}")
            t_xn, t_sd = emit_ln(xt, xnnat[:], [t_ld, tok_xnnat, tok_stats, T_W_V], f"A1_{tt}")
            tok_stats = t_sd
            tok_xtile[buf] = t_xn
            for q4 in range(4):
                b = 4 + q4 % 2
                for k4 in range(4):
                    i = q4 * 4 + k4
                    t_mm = pr.emit("tensor", lambda t, i=i, b=b, k4=k4: t.matmul(
                        psA[b][:, k4 * P:(k4 + 1) * P], xnnat[:, i * P:(i + 1) * P], ident[:],
                        start=True, stop=True),
                        waits=[t_xn, T_ID] + ([bk.gate(b)] if k4 == 0 else []),
                        inc="pe", tag=f"A1T{tt}_{q4}_{k4}")
                def eva1(v, q4=q4, b=b, tt=tt):
                    d3 = xnT[:, q4 * 4 * TT:(q4 * 4 + 4) * TT] \
                        .rearrange("p (i r) -> p i r", i=4)[:, :, tt * P:(tt + 1) * P]
                    return v.tensor_copy(d3, psA[b][:, 0:512].rearrange("p (i r) -> p i r", i=4))
                t_ev = pr.emit("vector", eva1, waits=[t_mm], inc="v", tag=f"A1E{tt}_{q4}")
                bk.done(b, t_ev)
            tok_xnnat = t_mm
            pump(1)
        TOK_XNT = pr.marker("vector")

        ap = dbg_out("xnT", [P, DT * TT], BF16)
        if ap is not None:
            pr.dma("sync", ap, xnT, waits=[TOK_XNT], tag="dbgxnT")

        tok_wrow = [None, None]
        tok_wrb = [None, None]
        tok_qkv = {"q": None, "k": None, "v": None}
        tok_expb = [None, None]
        tok_rbc = None
        t_ao = None

        for g in range(G):
            for which, wsrc, dstbuf in (("q", d_in["wq"], qT), ("k", d_in["wk"], kT)):
                for pt in range(2):
                    buf = pt % 2
                    wr = wrow[:, buf * D:(buf + 1) * D]
                    wrb = wrow_b[:, buf * D:(buf + 1) * D]
                    r0 = DPG * g + P * pt
                    t_ld = pr.dma("sync", wr, wsrc[r0:r0 + P, :], waits=[tok_wrow[buf]],
                                  tag=f"A2ld{which}{g}{pt}", stream=f"wr{buf}")
                    t_cast = pr.emit("scalar", lambda s, wr=wr, wrb=wrb: s.copy(wrb, wr),
                                     waits=[t_ld, tok_wrb[buf]], inc="s")
                    tok_wrow[buf] = t_cast
                    ev_toks = []
                    for q4 in range(4):
                        b = q4 % 2
                        for k4 in range(4):
                            i = q4 * 4 + k4
                            t_mm = pr.emit("tensor", lambda t, wrb=wrb, i=i, b=b, k4=k4: t.matmul(
                                psA[b][:, k4 * P:(k4 + 1) * P], wrb[:, i * P:(i + 1) * P], ident[:],
                                start=True, stop=True),
                                waits=[t_cast, T_ID] + ([bk.gate(b)] if k4 == 0 else []),
                                inc="pe", tag=f"A2T{which}{g}{pt}_{i}")
                        t_ev = pr.emit("vector", lambda v, q4=q4, b=b: v.tensor_copy(
                            wT_a[:, q4 * 512:(q4 + 1) * 512], psA[b][:, 0:512]),
                            waits=[t_mm], inc="v")
                        bk.done(b, t_ev)
                        ev_toks.append(t_ev)
                    tok_wrb[buf] = t_mm
                    if g == G - 1 and which == "q" and pt == 0:
                        ap = dbg_out("wTa", [P, DT * P], BF16)
                        if ap is not None:
                            pr.dma("sync", ap, wT_a, waits=[ev_toks[-1]], tag="dbgwTa")
                    last_c = {}
                    for i in range(DT):
                        for ci, (c0, cw) in enumerate(TCH):
                            b = 2 + ci
                            t_mm = pr.emit("tensor", lambda t, i=i, b=b, c0=c0, cw=cw: t.matmul(
                                psA[b][:, 0:cw], wT_a[:, i * P:(i + 1) * P],
                                xnT[:, i * TT + c0: i * TT + c0 + cw],
                                start=(i == 0), stop=(i == DT - 1)),
                                waits=([ev_toks[-1], TOK_XNT, bk.gate(b)] if i == 0 else []),
                                inc="pe", tag=f"A2M{which}{g}{pt}_{c0}_{i}")
                            last_c[ci] = t_mm
                    for ci, (c0, cw) in enumerate(TCH):
                        b = 2 + ci
                        t_ev = pr.emit("vector", lambda v, b=b, pt=pt, c0=c0, cw=cw, dstbuf=dstbuf:
                                       v.tensor_copy(dstbuf[:, pt * TT + c0: pt * TT + c0 + cw],
                                                     psA[b][:, 0:cw]),
                                       waits=[last_c[ci]], inc="v", tag=f"A2E{which}{g}{pt}_{c0}")
                        bk.done(b, t_ev)
                    pump(3)
                tok_qkv[which] = t_ev
            ev_toks_v = []
            for pt in range(2):
                buf = pt % 2
                wr = wrow[:, buf * D:(buf + 1) * D]
                wrb = wrow_b[:, buf * D:(buf + 1) * D]
                r0 = DPG * g + P * pt
                t_ld = pr.dma("sync", wr, d_in["wv"][r0:r0 + P, :], waits=[tok_wrow[buf]],
                              tag=f"A2ldv{g}{pt}", stream=f"wr{buf}")
                t_cast = pr.emit("scalar", lambda s, wr=wr, wrb=wrb: s.copy(wrb, wr),
                                 waits=[t_ld, tok_wrb[buf]], inc="s")
                tok_wrow[buf] = t_cast
                for q4 in range(4):
                    b = q4 % 2
                    for k4 in range(4):
                        i = q4 * 4 + k4
                        t_mm = pr.emit("tensor", lambda t, wrb=wrb, i=i, b=b, k4=k4: t.matmul(
                            psA[b][:, k4 * P:(k4 + 1) * P], wrb[:, i * P:(i + 1) * P], ident[:],
                            start=True, stop=True),
                            waits=[t_cast, T_ID] + ([bk.gate(b)] if k4 == 0 else []),
                            inc="pe", tag=f"A2Tv{g}{pt}_{i}")
                    def evv(v, q4=q4, pt=pt, b=b):
                        d3 = wT_v[:, q4 * 4 * DPG:(q4 * 4 + 4) * DPG] \
                            .rearrange("p (i r) -> p i r", i=4)[:, :, pt * P:(pt + 1) * P]
                        return v.tensor_copy(d3, psA[b][:, 0:512].rearrange("p (i r) -> p i r", i=4))
                    t_ev = pr.emit("vector", evv, waits=[t_mm], inc="v")
                    bk.done(b, t_ev)
                    ev_toks_v.append(t_ev)
                tok_wrb[buf] = t_mm
            for tt in range(TTILES):
                b = 2 + tt % 4
                for i in range(DT):
                    t_mm = pr.emit("tensor", lambda t, i=i, b=b, tt=tt: t.matmul(
                        psA[b][:, 0:DPG], xnT[:, i * TT + tt * P: i * TT + (tt + 1) * P],
                        wT_v[:, i * DPG:(i + 1) * DPG],
                        start=(i == 0), stop=(i == DT - 1)),
                        waits=([ev_toks_v[-1], TOK_XNT, bk.gate(b)] if i == 0 else []),
                        inc="pe", tag=f"A2Mv{g}{tt}_{i}")
                t_ev = pr.emit("vector", lambda v, b=b, tt=tt: v.tensor_copy(
                    vN[:, tt * DPG:(tt + 1) * DPG], psA[b][:, 0:DPG]),
                    waits=[t_mm], inc="v", tag=f"A2Ev{g}{tt}")
                bk.done(b, t_ev)
                pump(1)
            tok_qkv["v"] = t_ev
            T_QKV = [tok_qkv["q"], tok_qkv["k"], tok_qkv["v"]]

            if g == G - 1:
                for nm, bufp, sz in (("qT", qT, 2 * TT), ("kT", kT, 2 * TT),
                                     ("vN", vN, TTILES * DPG)):
                    ap = dbg_out(nm, [P, sz], BF16)
                    if ap is not None:
                        pr.dma("sync", ap, bufp, waits=T_QKV, tag="dbg" + nm)

            if g == 5 and "W" in phases:
                drain_w()
                T_W_REL = [wst["last_store"], wst["last_mm"], wst["last_sm"]]
            for ib in range(NB):
                msel = 0 if ib == 0 else 1
                t_den_mms = []
                t_av = {}
                for hp in range(HG):
                    base = DK * (hp % 2)
                    ptile = hp // 2
                    sbank = 4 + hp % 2
                    slot = hp % 2
                    exp_ap = expb[:, slot * 2 * P:(slot + 1) * 2 * P]
                    for cj in range(2):
                        def mm_sc(t, cj=cj, base=base, ptile=ptile, sbank=sbank, ib=ib):
                            return t.matmul(
                                psA[sbank][:, cj * P:(cj + 1) * P],
                                kT[base:base + DK,
                                   ptile * TT + (ib + cj) * P: ptile * TT + (ib + cj + 1) * P],
                                qT[base:base + DK,
                                   ptile * TT + (ib + 1) * P: ptile * TT + (ib + 2) * P],
                                start=True, stop=True, tile_position=(base, 0))
                        mm2 = pr.emit("tensor", mm_sc,
                                      waits=[bk.gate(sbank)] + (T_QKV if (ib == 0 and hp < 2) else []),
                                      inc="pe", tag=f"A3S{g}_{ib}_{hp}_{cj}")
                    t_exp = pr.emit("scalar", lambda s, exp_ap=exp_ap, sbank=sbank: s.activation(
                        exp_ap, psA[sbank][:, 0:2 * P], AF.Exp, bias=c20[:], scale=0.125),
                        waits=[mm2, T_C20, tok_expb[slot]], inc="s", tag=f"A3X{g}_{ib}_{hp}")
                    bk.done(sbank, t_exp)
                    t_msk = pr.emit("vector", lambda v, exp_ap=exp_ap, msel=msel: v.tensor_tensor(
                        exp_ap, exp_ap, maskTs[:, msel * 2 * P:(msel + 1) * 2 * P], ALU.mult),
                        waits=[t_exp, T_MSK0, T_MSK1], inc="v", tag=f"A3M{g}_{ib}_{hp}")
                    for cj in range(2):
                        t_den = pr.emit("tensor", lambda t, cj=cj, hp=hp, exp_ap=exp_ap: t.matmul(
                            psA[6][0:1, hp * P:(hp + 1) * P], ones_col[:],
                            exp_ap[:, cj * P:(cj + 1) * P],
                            start=(cj == 0), stop=(cj == 1)),
                            waits=[t_msk] + ([bk.gate(6)] if (cj == 0 and hp == 0) else []),
                            inc="pe", tag=f"A3D{g}_{ib}_{hp}_{cj}")
                    t_den_mms.append(t_den)
                    vbank = 7 if hp < 2 else 3
                    for cj in range(2):
                        def mm_av(t, cj=cj, hp=hp, base=base, ib=ib, vbank=vbank, exp_ap=exp_ap):
                            return t.matmul(
                                psA[vbank][base:base + DK, 0:P],
                                vN[:, (ib + cj) * DPG + hp * DK: (ib + cj) * DPG + (hp + 1) * DK],
                                exp_ap[:, cj * P:(cj + 1) * P],
                                start=(cj == 0), stop=(cj == 1),
                                tile_position=(0, base))
                        t_avm = pr.emit("tensor", mm_av,
                                        waits=[t_msk] + ([bk.gate(vbank)] if (cj == 0 and hp % 2 == 0) else []),
                                        inc="pe", tag=f"A3V{g}_{ib}_{hp}_{cj}")
                    t_av[hp] = t_avm
                    tok_expb[slot] = t_avm
                t_rec = pr.emit("vector", lambda v: v.reciprocal(recip[:], psA[6][0:1, 0:HG * P]),
                                waits=t_den_mms, inc="v", tag=f"A3R{g}_{ib}")
                bk.done(6, t_rec)
                for p2 in range(2):
                    vbank = 7 if p2 == 0 else 3
                    for par in range(2):
                        hp = 2 * p2 + par
                        def mm_bc(t, par=par, hp=hp, vbank=vbank):
                            return t.matmul(
                                psA[vbank][par * DK:(par + 1) * DK, P:2 * P],
                                ones_r64[:], recip[0:1, hp * P:(hp + 1) * P],
                                start=True, stop=True, tile_position=(0, par * DK))
                        t_bc = pr.emit("tensor", mm_bc, waits=[t_rec], inc="pe",
                                       tag=f"A3B{g}_{ib}_{hp}")
                    rb = rbc[:, p2 * P:(p2 + 1) * P]
                    t_rbc = pr.emit("vector", lambda v, vbank=vbank, rb=rb: v.tensor_copy(
                        rb, psA[vbank][:, P:2 * P]),
                        waits=[t_bc, tok_rbc], inc="v")
                    dto = 2 * g + p2
                    t_ao = pr.emit("vector", lambda v, rb=rb, vbank=vbank, dto=dto, ib=ib:
                                   v.tensor_tensor(
                                       aoT[:, dto * TLOC + ib * P: dto * TLOC + (ib + 1) * P],
                                       psA[vbank][:, 0:P], rb, ALU.mult),
                                   waits=[t_av[2 * p2], t_av[2 * p2 + 1], F(t_rbc)] +
                                         (T_W_REL if dto >= 11 else []), inc="v",
                                   tag=f"A3O{g}_{ib}_{p2}")
                    bk.done(vbank, t_ao)
                    tok_rbc = t_ao
                pump(3)

        T_AOT = pr.marker("vector")
        ap = dbg_out("aoT", [P, DT * TLOC], BF16)
        if ap is not None:
            pr.dma("sync", ap, aoT, waits=[T_AOT], tag="dbgaoT")

        tok_projb = None
        tok_xrld = None
        last_add = None
        for dj in range(DT):
            buf = dj % 2
            wr = wrow[:, buf * D:(buf + 1) * D]
            wrb = wrow_b[:, buf * D:(buf + 1) * D]
            t_ld = pr.dma("sync", wr, d_in["wo"][dj * P:(dj + 1) * P, :],
                          waits=[tok_wrow[buf]], tag=f"A4ld{dj}", stream=f"wr{buf}")
            t_cast = pr.emit("scalar", lambda s, wr=wr, wrb=wrb: s.copy(wrb, wr),
                             waits=[t_ld, tok_wrb[buf]], inc="s")
            tok_wrow[buf] = t_cast
            ev_toks = []
            for m in range(DT):
                b = m % 2
                t_mm = pr.emit("tensor", lambda t, wrb=wrb, m=m, b=b: t.matmul(
                    psA[b][:, 0:P], wrb[:, m * P:(m + 1) * P], ident[:],
                    start=True, stop=True),
                    waits=[t_cast, T_ID, bk.gate(b)], inc="pe", tag=f"A4T{dj}_{m}")
                t_ev = pr.emit("vector", lambda v, m=m, b=b: v.tensor_copy(
                    wT_a[:, m * P:(m + 1) * P], psA[b][:, 0:P]),
                    waits=[t_mm], inc="v")
                bk.done(b, t_ev)
                ev_toks.append(t_ev)
            tok_wrb[buf] = t_mm
            for tc in range(4):
                b = 2
                for m in range(DT):
                    t_mm = pr.emit("tensor", lambda t, m=m, tc=tc, b=b: t.matmul(
                        psA[b][:, 0:512], wT_a[:, m * P:(m + 1) * P],
                        aoT[:, m * TLOC + tc * 512: m * TLOC + (tc + 1) * 512],
                        start=(m == 0), stop=(m == DT - 1)),
                        waits=([ev_toks[-1], T_AOT, bk.gate(b)] if m == 0 else []),
                        inc="pe", tag=f"A4M{dj}_{tc}_{m}")
                t_pb = pr.emit("scalar", lambda s, b=b: s.copy(projb[:], psA[b][:, 0:512]),
                               waits=[t_mm, tok_projb], inc="s", tag=f"A4P{dj}_{tc}")
                bk.done(b, t_pb)
                t_xr = pr.dma("sync", xrld[:],
                              d_in["x_loc"][tc * 512:(tc + 1) * 512, dj * P:(dj + 1) * P]
                              .rearrange("(a p) b -> p a b", p=P),
                              waits=[tok_xrld], tag=f"A4xr{dj}_{tc}", stream="xr")
                for s4 in range(4):
                    b2 = 4 + s4
                    t_mm2 = pr.emit("tensor", lambda t, s4=s4, b2=b2: t.matmul(
                        psA[b2][:, 0:P], projb[:, s4 * P:(s4 + 1) * P], ident[:],
                        start=True, stop=True),
                        waits=[t_pb, bk.gate(b2)], inc="pe", tag=f"A4N{dj}_{tc}_{s4}")
                    tti = tc * 4 + s4
                    t_add = pr.emit("vector", lambda v, s4=s4, b2=b2, tti=tti, dj=dj:
                                    v.tensor_tensor(
                                        x2[:, tti * D + dj * P: tti * D + (dj + 1) * P],
                                        psA[b2][:, 0:P], xrld[:, s4 * P:(s4 + 1) * P], ALU.add),
                                    waits=[t_mm2, t_xr], inc="v",
                                    tag=f"A4A{dj}_{tc}_{s4}")
                    bk.done(b2, t_add)
                    last_add = t_add
                tok_projb = t_mm2
                tok_xrld = last_add
        T_X2 = pr.marker("vector")
        ap = dbg_out("x2", [P, NB * D], BF16)
        if ap is not None:
            pr.dma("sync", ap, x2, waits=[T_X2], tag="dbgx2")

    if "W" in phases:
        drain_w()
        T_W_DONE = [wst["last_store"]]
        for nm, scr_, shp in (("scrgZ", scr_g, [D, NF]), ("scruZ", scr_u, [D, NF]),
                              ("scrdZ", scr_d, [NF, D])):
            ap = dbg_out(nm, shp, BF16)
            if ap is not None:
                pr.dma("sync", ap, scr_[:, :], waits=T_W_DONE, tag="dbg" + nm)
    else:
        T_W_DONE = []
    if "B" in phases:
        st_toks = []
        tok_wgt = [None, None]
        tok_wut = [None, None]
        tok_wdt = [None, None]
        tok_g1 = None
        tok_xn2 = None
        tok_otile = None
        for slab in range(2):
            tok_acc = {}
            for u in range(8):
                tti = slab * 8 + u
                x2t = x2[:, tti * D:(tti + 1) * D]
                t_xn, t_sd = emit_ln(x2t, xnnat[:], [T_X2, tok_xn2, tok_stats], f"B1_{slab}_{u}")
                tok_stats = t_sd
                for q4 in range(4):
                    b = q4 % 2
                    for k4 in range(4):
                        i = q4 * 4 + k4
                        t_mm = pr.emit("tensor", lambda t, i=i, b=b, k4=k4: t.matmul(
                            psA[b][:, k4 * P:(k4 + 1) * P], xnnat[:, i * P:(i + 1) * P], ident[:],
                            start=True, stop=True),
                            waits=[t_xn, T_ID] + ([bk.gate(b)] if k4 == 0 else []),
                            inc="pe", tag=f"B1T{slab}_{u}_{q4}_{k4}")
                    def evx(v, q4=q4, b=b, u=u):
                        d3 = xn2T[:, q4 * 4 * SLAB:(q4 * 4 + 4) * SLAB] \
                            .rearrange("p (i r) -> p i r", i=4)[:, :, u * P:(u + 1) * P]
                        return v.tensor_copy(d3, psA[b][:, 0:512].rearrange("p (i r) -> p i r", i=4))
                    t_ev = pr.emit("vector", evx, waits=[t_mm], inc="v")
                    bk.done(b, t_ev)
                tok_xn2 = t_mm
                for q4 in range(4):
                    b = 4 + q4 % 2
                    for k4 in range(4):
                        i = q4 * 4 + k4
                        t_mm = pr.emit("tensor", lambda t, i=i, b=b, k4=k4, x2t=x2t: t.matmul(
                            psA[b][:, k4 * P:(k4 + 1) * P], x2t[:, i * P:(i + 1) * P], ident[:],
                            start=True, stop=True),
                            waits=[T_X2, T_ID] + ([bk.gate(b)] if k4 == 0 else []),
                            inc="pe", tag=f"B1A{slab}_{u}_{q4}_{k4}")
                    def eva(v, q4=q4, b=b, u=u):
                        d3 = aoT[:, q4 * 4 * SLAB:(q4 * 4 + 4) * SLAB] \
                            .rearrange("p (i r) -> p i r", i=4)[:, :, u * P:(u + 1) * P]
                        return v.tensor_copy(d3, psA[b][:, 0:512].rearrange("p (i r) -> p i r", i=4))
                    t_ev = pr.emit("vector", eva, waits=[t_mm], inc="v",
                                   tag=f"B1AE{slab}_{u}_{q4}")
                    bk.done(b, t_ev)
            TOK_XN2 = pr.marker("vector")
            for nm in (f"acc{slab}i", f"xn2{slab}"):
                ap = dbg_out(nm, [P, ACCOFF], BF16)
                if ap is not None:
                    pr.dma("sync", ap, aoT[:, 0:ACCOFF] if nm.startswith("acc") else xn2T,
                           waits=[TOK_XN2], tag="dbg" + nm)

            for sup in range(8):
                inter_toks = []
                for ch in range(16):
                    c0 = sup * 2048 + ch * P
                    buf = ch % 2
                    wg = wgt[:, buf * 2048:(buf + 1) * 2048]
                    wu = wut[:, buf * 2048:(buf + 1) * 2048]
                    t_wg = pr.dma("sync", wg,
                                  scr_g[:, c0:c0 + P].rearrange("(i p) n -> p i n", p=P),
                                  waits=[tok_wgt[buf], T_X2] + T_W_DONE,
                                  tag=f"B2g{slab}_{sup}_{ch}", stream=f"bg{buf}")
                    t_wu = pr.dma("gpsimd", wu,
                                  scr_u[:, c0:c0 + P].rearrange("(i p) n -> p i n", p=P),
                                  waits=[tok_wut[buf], T_X2] + T_W_DONE,
                                  tag=f"B2u{slab}_{sup}_{ch}", stream=f"bu{buf}")
                    ntl = ch
                    lastg = {}
                    lastu = {}
                    for i in range(DT):
                        for tc in range(2):
                            t_mmg = pr.emit("tensor", lambda t, i=i, tc=tc, wg=wg: t.matmul(
                                psA[tc][:, 0:512], wg[:, i * P:(i + 1) * P],
                                xn2T[:, i * SLAB + tc * 512: i * SLAB + (tc + 1) * 512],
                                start=(i == 0), stop=(i == DT - 1)),
                                waits=([t_wg, TOK_XN2, bk.gate(tc)] if i == 0 else []),
                                inc="pe", tag=f"B2G{slab}_{sup}_{ntl}_{tc}_{i}")
                            lastg[tc] = t_mmg
                    for i in range(DT):
                        for tc in range(2):
                            t_mmu = pr.emit("tensor", lambda t, i=i, tc=tc, wu=wu: t.matmul(
                                psA[2 + tc][:, 0:512], wu[:, i * P:(i + 1) * P],
                                xn2T[:, i * SLAB + tc * 512: i * SLAB + (tc + 1) * 512],
                                start=(i == 0), stop=(i == DT - 1)),
                                waits=([t_wu, TOK_XN2, bk.gate(2 + tc)] if i == 0 else []),
                                inc="pe", tag=f"B2U{slab}_{sup}_{ntl}_{tc}_{i}")
                            lastu[tc] = t_mmu
                    for tc in range(2):
                        t_gelu = pr.emit("scalar", lambda s, tc=tc: s.activation(
                            g1[:], psA[tc][:, 0:512], AF.Gelu_apprx_tanh),
                            waits=[lastg[tc], tok_g1], inc="s", tag=f"B2A{slab}_{sup}_{ntl}_{tc}")
                        bk.done(tc, t_gelu)
                        t_int = pr.emit("vector", lambda v, ntl=ntl, tc=tc: v.tensor_tensor(
                            aoT[:, ACCOFF + ntl * SLAB + tc * 512: ACCOFF + ntl * SLAB + (tc + 1) * 512],
                            psA[2 + tc][:, 0:512], g1[:], ALU.mult),
                            waits=[lastu[tc], t_gelu], inc="v", tag=f"B2I{slab}_{sup}_{ntl}_{tc}")
                        bk.done(2 + tc, t_int)
                        tok_g1 = t_int
                        inter_toks.append(t_int)
                    tok_wgt[buf] = lastg[1]
                    tok_wut[buf] = lastu[1]
                for ch in range(16):
                    buf = ch % 2
                    wd = wdt[:, buf * 2048:(buf + 1) * 2048]
                    t_wd = pr.dma("gpsimd", wd,
                                  scr_d[sup * 2048:(sup + 1) * 2048, ch * P:(ch + 1) * P]
                                  .rearrange("(i p) n -> p i n", p=P),
                                  waits=[tok_wdt[buf], T_X2] + T_W_DONE,
                                  tag=f"B3d{slab}_{sup}_{ch}", stream=f"bd{buf}")
                    hti = ch
                    lastd = {}
                    for i in range(DT):
                        for tc in range(2):
                            b = 4 + 2 * (ch % 2) + tc
                            t_mmd = pr.emit("tensor", lambda t, i=i, tc=tc, b=b, wd=wd: t.matmul(
                                psA[b][:, 0:512], wd[:, i * P:(i + 1) * P],
                                aoT[:, ACCOFF + i * SLAB + tc * 512: ACCOFF + i * SLAB + (tc + 1) * 512],
                                start=(i == 0), stop=(i == DT - 1)),
                                waits=([t_wd, bk.gate(b)] + inter_toks if i == 0 else []),
                                inc="pe", tag=f"B3M{slab}_{sup}_{hti}_{tc}_{i}")
                            lastd[tc] = t_mmd
                    if slab == 1 and sup == 0 and ch == 0:
                        ap = dbg_out("dpsum", [P, 512], F32)
                        if ap is not None:
                            t_cp = pr.emit("vector", lambda v: v.tensor_copy(
                                otile[:, 0:512], psA[4][:, 0:512]),
                                waits=[lastd[0], lastd[1]], inc="v", tag="dbgdpsumcp")
                            pr.dma("sync", ap, otile[:, 0:512], waits=[t_cp], tag="dbgdpsum")
                        ap = dbg_out("dwdt", [P, DT * P], BF16)
                        if ap is not None:
                            t_cpw = pr.emit("vector", lambda v, wd=wd: v.tensor_copy(
                                xnnat[:], wd), waits=[t_wd], inc="v", tag="dbgdwdtcp")
                            pr.dma("sync", ap, xnnat[:], waits=[t_cpw], tag="dbgdwdt")
                    for tc in range(2):
                        b = 4 + 2 * (ch % 2) + tc
                        key = (hti, tc)
                        t_acc = pr.emit("vector", lambda v, hti=hti, tc=tc, b=b: v.tensor_tensor(
                            aoT[:, hti * SLAB + tc * 512: hti * SLAB + (tc + 1) * 512],
                            psA[b][:, 0:512],
                            aoT[:, hti * SLAB + tc * 512: hti * SLAB + (tc + 1) * 512],
                            ALU.add),
                            waits=[lastd[tc], tok_acc.get(key)], inc="v",
                            tag=f"B3A{slab}_{sup}_{hti}_{tc}")
                        bk.done(b, t_acc)
                        tok_acc[key] = t_acc
                    tok_wdt[buf] = lastd[1]
                for nm, rg0, rg1 in ((f"accS{slab}_{sup}", 0, ACCOFF),
                                     (f"intS{slab}_{sup}", ACCOFF, 2 * ACCOFF)):
                    ap = dbg_out(nm, [P, ACCOFF], BF16)
                    if ap is not None:
                        toks = [tok_acc[kk2] for kk2 in tok_acc]
                        pr.dma("sync", ap, aoT[:, rg0:rg1], waits=toks, tag="dbg" + nm)
            for u in range(8):
                for half in range(2):
                    ev_last = None
                    for q4 in range(2):
                        b = q4 % 2
                        for k4 in range(4):
                            i = half * 8 + q4 * 4 + k4
                            t_mm = pr.emit("tensor", lambda t, i=i, u=u, b=b, k4=k4: t.matmul(
                                psA[b][:, k4 * P:(k4 + 1) * P],
                                aoT[:, i * SLAB + u * P: i * SLAB + (u + 1) * P],
                                ident[:], start=True, stop=True),
                                waits=[tok_acc.get((i, u // 4)), T_ID] +
                                      ([bk.gate(b), tok_otile] if k4 == 0 else []),
                                inc="pe", tag=f"B4T{slab}_{u}_{i}")
                        t_ev = pr.emit("vector", lambda v, q4=q4, b=b: v.tensor_copy(
                            otile[:, q4 * 512:(q4 + 1) * 512], psA[b][:, 0:512]),
                            waits=[t_mm], inc="v")
                        bk.done(b, t_ev)
                        ev_last = t_ev
                    t_st = pr.dma("sync",
                                  out_t[slab * SLAB + u * P: slab * SLAB + (u + 1) * P,
                                        half * 1024:(half + 1) * 1024],
                                  otile[:], waits=[ev_last], tag=f"B4S{slab}_{u}_{half}",
                                  stream="bst")
                    st_toks.append(t_st)
                    tok_otile = t_st
        pr.emit("gpsimd", lambda gq: gq.engine_nop(), waits=[st_toks[-1]], inc="g", tag="final")
    else:
        allw = [(s, pr.cnt[s], "dma") for s in pr.cnt if s.startswith("dma_") and pr.cnt[s]]
        pr.emit("gpsimd", lambda gq: gq.engine_nop(), waits=allw, inc="g", tag="final")

    return nc, ctx, pr, d_in, out_t, dbg


def finish(nc, ctx, pr):
    block = ctx.enter_context(nc.Block())
    pr.build(block)
    ctx.close()
    return nc


from concourse.bass_utils import run_bass_kernel_spmd
import ml_dtypes

NCORES = 8

_mask_cache = {}


def _masks():
    if "m" not in _mask_cache:
        r = np.arange(P)[None, :]
        c = np.arange(2 * P)[:, None]
        rest = ((c >= r) & (c <= r + P)).astype(np.float32)
        first = rest * (c >= P)
        _mask_cache["m"] = (first.astype(ml_dtypes.bfloat16),
                           rest.astype(ml_dtypes.bfloat16))
    return _mask_cache["m"]


_prog_cache = {}


def _get_nc():
    if "nc" not in _prog_cache:
        nc, ctx, pr, d_in, out_t, dbg = build_program(phases=("W", "A", "B"), debug=())
        block = ctx.enter_context(nc.Block())
        pr.build(block)
        ctx.close()
        _prog_cache["nc"] = nc
    return _prog_cache["nc"]


def kernel(L0_attn_x, L0_attn_Wq, L0_attn_Wk, L0_attn_Wv, L0_attn_Wo,
           L0_ffn_gate_W_packed, L0_ffn_gate_scales,
           L0_ffn_up_W_packed, L0_ffn_up_scales,
           L0_ffn_down_W_packed, L0_ffn_down_scales):
    x = np.ascontiguousarray(np.asarray(L0_attn_x, dtype=np.float32).reshape(NCORES * TLOC, D))
    wq = np.ascontiguousarray(np.asarray(L0_attn_Wq, dtype=np.float32))
    wk = np.ascontiguousarray(np.asarray(L0_attn_Wk, dtype=np.float32))
    wv = np.ascontiguousarray(np.asarray(L0_attn_Wv, dtype=np.float32))
    wo = np.ascontiguousarray(np.asarray(L0_attn_Wo, dtype=np.float32))
    def _bytes(a):
        a = np.ascontiguousarray(np.asarray(a, dtype=np.int32))
        return np.ascontiguousarray(a.view(np.uint8).reshape(a.shape[0], -1)[:, 0::4])
    gp = _bytes(L0_ffn_gate_W_packed)
    gs = np.ascontiguousarray(np.asarray(L0_ffn_gate_scales, dtype=np.float32))
    up = _bytes(L0_ffn_up_W_packed)
    us = np.ascontiguousarray(np.asarray(L0_ffn_up_scales, dtype=np.float32))
    dp = _bytes(L0_ffn_down_W_packed)
    ds = np.ascontiguousarray(np.asarray(L0_ffn_down_scales, dtype=np.float32))
    mfirst, mrest = _masks()
    zero_halo = np.zeros((HALO, D), dtype=np.float32)

    in_maps = []
    for c in range(NCORES):
        halo = zero_halo if c == 0 else x[c * TLOC - HALO:c * TLOC]
        in_maps.append(dict(
            x_loc=x[c * TLOC:(c + 1) * TLOC],
            x_halo=np.ascontiguousarray(halo),
            wq=wq, wk=wk, wv=wv, wo=wo,
            gate_p=gp, gate_s=gs, up_p=up, up_s=us, down_p=dp, down_s=ds,
            maskT_first=(mfirst if c == 0 else mrest),
            maskT_rest=mrest,
        ))

    nc = _get_nc()
    res = run_bass_kernel_spmd(nc, in_maps, core_ids=list(range(NCORES)))
    out = np.concatenate([np.asarray(r["out"], dtype=np.float32) for r in res.results], axis=0)
    return out.reshape(1, NCORES * TLOC, D)


# version 12
# speedup vs baseline: 1.1862x; 1.1862x over previous
import sys
if "/opt/trn_rl_repo" not in sys.path:
    sys.path.insert(0, "/opt/trn_rl_repo")
import numpy as np
from contextlib import ExitStack

import concourse.bass as bass
import concourse.mybir as mybir

F32 = mybir.dt.float32
BF16 = mybir.dt.bfloat16
I16 = mybir.dt.int16
I32 = mybir.dt.int32
U8 = mybir.dt.uint8
U16 = mybir.dt.uint16
AF = mybir.ActivationFunctionType
ALU = mybir.AluOpType
AX = mybir.AxisListType

P = 128
D = 2048
DT = D // P
TLOC = 2048
HALO = 128
TT = TLOC + HALO
TTILES = TT // P
NB = TLOC // P
H = 32
DK = 64
HG = 4
G = H // HG
DPG = HG * DK
NF = 16384
EXP_OFF = -20.0
SLAB = 1024
ACCOFF = DT * SLAB

TCH = [(0, 512), (512, 512), (1024, 512), (1536, 512), (2048, 128)]


class Prog:
    def __init__(self, nc, ctx):
        self.nc = nc
        self.ctx = ctx
        self.ops = []
        self.sems = {}
        self.cnt = {}

    def sem(self, name):
        if name not in self.sems:
            self.sems[name] = self.ctx.enter_context(self.nc.semaphore(name))
            self.cnt[name] = 0
        return self.sems[name]

    def emit(self, engine, fn, waits=(), inc=None, inc_by=1, tag=""):
        w = []
        for t in waits:
            if t is None:
                continue
            s, v, peng = t
            if peng == engine and not s.startswith("dma"):
                continue
            w.append((s, v))
        wd = {}
        for s, v in w:
            wd[s] = max(wd.get(s, 0), v)
        tok = None
        if inc is not None:
            self.sem(inc)
            self.cnt[inc] += inc_by
            tok = (inc, self.cnt[inc], engine if inc_by == 1 else "dma")
        self.ops.append((engine, fn, sorted(wd.items()), inc, inc_by, tag))
        return tok

    def marker(self, engine="vector"):
        return self.emit(engine, lambda e: e.engine_nop(), inc="mark", tag="marker")

    def dma(self, queue, out, in_, waits=(), tag="", stream=None):
        sname = "dma_" + (stream or queue)
        self.sem(sname)
        w = list(waits)
        if self.cnt[sname]:
            w.append((sname, self.cnt[sname], "dma"))

        def fn(eng):
            return eng.dma_start(out=out, in_=in_)
        return self.emit(queue, fn, w, inc=sname, inc_by=16, tag=tag)

    def build(self, block):
        sems = self.sems

        def make(engine_name):
            def closure(eng):
                for (e, fn, waits, inc, inc_by, tag) in self.ops:
                    if e != engine_name:
                        continue
                    for (s, v) in waits:
                        eng.wait_ge(sems[s], v)
                    inst = fn(eng)
                    if inc is not None:
                        assert inst is not None, f"op {tag} returned no instruction"
                        inst.then_inc(sems[inc], inc_by)
            return closure

        block.sync(make("sync"))
        block.scalar(make("scalar"))
        block.vector(make("vector"))
        block.tensor(make("tensor"))
        block.gpsimd(make("gpsimd"))


def F(tok):
    return None if tok is None else (tok[0], tok[1], "__force__")


class Banks:
    def __init__(self, n=8):
        self.tok = [None] * n

    def gate(self, b):
        return self.tok[b]

    def done(self, b, token):
        self.tok[b] = token


def build_program(phases=("W", "A", "B"), debug=()):
    nc = bass.Bass("TRN2", target_bir_lowering=False, debug=False)
    ctx = ExitStack()

    d_in = {}
    d_in["x_loc"] = nc.dram_tensor("x_loc", [TLOC, D], F32, kind="ExternalInput").ap()
    d_in["x_halo"] = nc.dram_tensor("x_halo", [HALO, D], F32, kind="ExternalInput").ap()
    for w in ("wq", "wk", "wv", "wo"):
        d_in[w] = nc.dram_tensor(w, [D, D], F32, kind="ExternalInput").ap()
    d_in["gate_p"] = nc.dram_tensor("gate_p", [NF, D // 2], U16, kind="ExternalInput").ap()
    d_in["gate_s"] = nc.dram_tensor("gate_s", [NF, D // 32], F32, kind="ExternalInput").ap()
    d_in["up_p"] = nc.dram_tensor("up_p", [NF, D // 2], U16, kind="ExternalInput").ap()
    d_in["up_s"] = nc.dram_tensor("up_s", [NF, D // 32], F32, kind="ExternalInput").ap()
    d_in["down_p"] = nc.dram_tensor("down_p", [D, NF // 2], U16, kind="ExternalInput").ap()
    d_in["down_s"] = nc.dram_tensor("down_s", [D, NF // 32], F32, kind="ExternalInput").ap()
    d_in["maskT_first"] = nc.dram_tensor("maskT_first", [2 * P, P], BF16, kind="ExternalInput").ap()
    d_in["maskT_rest"] = nc.dram_tensor("maskT_rest", [2 * P, P], BF16, kind="ExternalInput").ap()

    out_t = nc.dram_tensor("out", [TLOC, D], F32, kind="ExternalOutput").ap()

    scr_g = nc.dram_tensor("scr_g", [D, NF], BF16, kind="Internal").ap()
    scr_u = nc.dram_tensor("scr_u", [D, NF], BF16, kind="Internal").ap()
    scr_d = nc.dram_tensor("scr_d", [NF, D], BF16, kind="Internal").ap()

    dbg = {}

    def dbg_out(name, shape, dt=F32):
        if name in debug:
            dbg[name] = nc.dram_tensor("dbg_" + name, shape, dt, kind="ExternalOutput").ap()
            return dbg[name]
        return None

    AR1 = ctx.enter_context(nc.sbuf_tensor("AR1", [P, 34816], BF16))
    AR2 = ctx.enter_context(nc.sbuf_tensor("AR2", [P, 32768], BF16))
    AR3 = ctx.enter_context(nc.sbuf_tensor("AR3", [P, 32768], BF16))

    def v1(a, b):
        return AR1[:, a:b]

    def v2(a, b):
        return AR2[:, a:b]

    def v3(a, b):
        return AR3[:, a:b]

    xnT = v1(0, 34816)
    x2 = v1(0, 32768)
    otile = v1(32768, 34816).bitcast(F32)
    aoT = v2(0, 32768)
    W_TAIL = 22912
    d_in16 = v2(22912, 24960).bitcast(U16)
    d_t1 = v2(24960, 25984).bitcast(U16)
    d_t2 = v2(25984, 27008).bitcast(U16)
    d_t1h = v2(27008, 28032).bitcast(U16)
    d_t2h = v2(28032, 29056).bitcast(U16)
    d_wn = v2(29056, 31104)
    d_sc = v2(31104, 31360).bitcast(F32)
    d_wt = v2(31360, 32384)
    d_scb = v2(32384, 32512)
    d_sce = v1(32768, 34816)
    xtile = v2(0, 8192).bitcast(F32)
    qT = v3(0, 4352)
    kT = v3(4352, 8704)
    vN = v3(8704, 13056)
    wrow = v3(13056, 21248).bitcast(F32)
    wrow_b = v3(21248, 25344)
    wT_a = v3(25344, 27392)
    wT_v = v3(27392, 31488)
    d_wt_hi = v3(31488, 32512)
    xn2T = v3(0, 16384)
    wgt = v3(16384, 20480)
    wut = v3(20480, 24576)
    wdt = v3(24576, 28672)
    xnnat = v3(28672, 30720)
    g1 = v3(30720, 31232)

    def gsb(name, pdim, fsize, dt):
        return ctx.enter_context(nc.sbuf_tensor(name, [pdim, fsize], dt))

    stats = gsb("stats", P, 16, F32)
    eps_t = gsb("eps_t", P, 1, F32)
    c20 = gsb("c20", P, 1, F32)
    ident = gsb("ident", P, P, BF16)
    identf = gsb("identf", P, P, F32)
    ones_col = gsb("ones_col", P, 1, BF16)
    ones_r64 = gsb("ones_r64", 1, DK, F32)
    maskTs = gsb("maskTs", P, 2 * 2 * P, BF16)
    expb = gsb("expb", P, 2 * 2 * P, BF16)
    recip = gsb("recip", 1, HG * P, F32)
    rbc = gsb("rbc", P, 2 * P, F32)
    projb = gsb("projb", P, 512, BF16)
    xrld = gsb("xrld", P, 4 * P, F32)

    psA = [ctx.enter_context(nc.psum_tensor(f"psA{i}", [P, 512], F32)) for i in range(8)]

    pr = Prog(nc, ctx)
    bk = Banks()

    t00 = pr.emit("gpsimd", lambda g: g.memset(identf[:], 0.0), inc="g")
    t0 = pr.emit("gpsimd", lambda g: g.affine_select(
        out=identf[:], in_=identf[:], compare_op=ALU.not_equal, fill=1.0, base=0,
        pattern=[[-1, P]], channel_multiplier=1), waits=[F(t00)], inc="g")
    T_ID = pr.emit("vector", lambda v: v.tensor_copy(ident[:], identf[:]), waits=[t0], inc="v")
    pr.emit("vector", lambda v: v.memset(ones_col[:], 1.0), inc="v")
    pr.emit("vector", lambda v: v.memset(ones_r64[:], 1.0), inc="v")
    pr.emit("vector", lambda v: v.memset(eps_t[:], 1e-5), inc="v")
    T_C20 = pr.emit("vector", lambda v: v.memset(c20[:], EXP_OFF), inc="v")
    T_MSK0 = pr.dma("sync", maskTs[:, 0:2 * P],
                    d_in["maskT_first"].rearrange("(a p) b -> p a b", p=P), tag="mskf")
    T_MSK1 = pr.dma("sync", maskTs[:, 2 * P:4 * P],
                    d_in["maskT_rest"].rearrange("(a p) b -> p a b", p=P), tag="mskr")

    W_UNITS = []
    if "W" in phases:
        for mat, sp, ss, n_rt, n_cc, scr in (
                ("g", d_in["gate_p"], d_in["gate_s"], NF // P, 1, scr_g),
                ("u", d_in["up_p"], d_in["up_s"], NF // P, 1, scr_u),
                ("d", d_in["down_p"], d_in["down_s"], D // P, 8, scr_d)):
            for rt in range(n_rt):
                for cc in range(n_cc):
                    W_UNITS.append((mat, sp, ss, scr, rt, cc))

    wst = {"ui": 0, "tok_in": [None, None], "tok_inr": [None, None],
           "tok_sc": [None, None], "tok_sm": [None, None],
           "tok_scb": [None, None],
           "tok_wn": [None, None], "tok_store": [None, None],
           "last_store": None, "last_mm": None, "last_sm": None}

    def emit_w_unit():
        ui = wst["ui"]
        if ui >= len(W_UNITS):
            return False
        wst["ui"] += 1
        mat, sp, ss, scr, rt, cc = W_UNITS[ui]
        buf = ui % 2
        inb = d_in16[:, buf * 1024:(buf + 1) * 1024]
        scb = d_sc[:, buf * 64:(buf + 1) * 64]
        rows = slice(rt * P, (rt + 1) * P)
        t_in = pr.dma("sync", inb, sp[rows, cc * 1024:(cc + 1) * 1024],
                      waits=[wst["tok_inr"][buf]], tag=f"Win{ui}", stream=f"wpk{buf}")
        t_sc = pr.dma("sync", scb, ss[rows, cc * 64:(cc + 1) * 64],
                      waits=[wst["tok_scb"][buf]], tag=f"Wsc{ui}", stream=f"wsc{buf}")
        scbb = d_scb[:, buf * 64:(buf + 1) * 64]
        scee = d_sce[:, buf * 1024:(buf + 1) * 1024]
        t_scb = pr.emit("scalar", lambda s, scbb=scbb, scb=scb: s.copy(scbb, scb),
                        waits=[t_sc, wst["tok_scb"][buf]], inc="s", tag=f"Wscb{ui}")
        t_sce = pr.emit("scalar", lambda s, scee=scee, scbb=scbb: s.copy(
            scee.rearrange("p (b r) -> p b r", r=16),
            scbb.unsqueeze(2).broadcast_to((P, 64, 16))),
            waits=[F(t_scb), wst["tok_sm"][buf]], inc="s", tag=f"Wsce{ui}")
        wst["tok_scb"][buf] = t_sce
        pr.emit("vector", lambda v, inb=inb: v.tensor_scalar(
            d_t1[:], inb, 7, 6, ALU.bitwise_and, ALU.logical_shift_left),
            waits=[t_in], inc="v", tag=f"Wc{ui}")
        pr.emit("vector", lambda v, inb=inb: v.tensor_scalar(
            d_t1h[:], inb, 112, 2, ALU.bitwise_and, ALU.logical_shift_left), inc="v")
        pr.emit("vector", lambda v, inb=inb: v.tensor_scalar(
            d_t2[:], inb, 8, 12, ALU.bitwise_and, ALU.logical_shift_left), inc="v")
        t_inr = pr.emit("vector", lambda v, inb=inb: v.tensor_scalar(
            d_t2h[:], inb, 128, 8, ALU.bitwise_and, ALU.logical_shift_left), inc="v")
        wst["tok_inr"][buf] = t_inr
        pr.emit("vector", lambda v: v.tensor_tensor(
            d_t2[:], d_t1[:], d_t2[:], ALU.add), inc="v")
        pr.emit("vector", lambda v: v.tensor_tensor(
            d_t2h[:], d_t1h[:], d_t2h[:], ALU.add), inc="v")
        pr.emit("vector", lambda v: v.tensor_single_scalar(
            d_t1[:], d_t1[:], 6, ALU.logical_shift_right), inc="v")
        pr.emit("vector", lambda v: v.tensor_single_scalar(
            d_t1h[:], d_t1h[:], 6, ALU.logical_shift_right), inc="v")
        pr.emit("vector", lambda v: v.tensor_scalar(
            d_t1[:], d_t1[:], 251, 252, ALU.mult, ALU.min), inc="v")
        pr.emit("vector", lambda v: v.tensor_scalar(
            d_t1h[:], d_t1h[:], 251, 252, ALU.mult, ALU.min), inc="v")
        pr.emit("vector", lambda v: v.tensor_single_scalar(
            d_t1[:], d_t1[:], 6, ALU.logical_shift_left), inc="v")
        pr.emit("vector", lambda v: v.tensor_single_scalar(
            d_t1h[:], d_t1h[:], 6, ALU.logical_shift_left), inc="v")
        t_el = pr.emit("vector", lambda v: v.tensor_tensor(
            d_t2[:], d_t1[:], d_t2[:], ALU.add), inc="v")
        t_eh = pr.emit("vector", lambda v: v.tensor_tensor(
            d_t2h[:], d_t1h[:], d_t2h[:], ALU.add), inc="v")
        sm_toks = []
        for koff, tsrc, t_e in ((0, d_t2, t_el), (1, d_t2h, t_eh)):
            patf = tsrc[:, :].bitcast(BF16)
            dst = d_wn[:, koff * 1024:(koff + 1) * 1024]
            t_sm = pr.emit("vector", lambda v, dst=dst, patf=patf, scee=scee:
                           v.tensor_tensor(dst, patf, scee, ALU.mult),
                           waits=[t_e, t_sce, wst["tok_wn"][koff]], inc="v",
                           tag=f"Wsm{ui}_{koff}")
            sm_toks.append(t_sm)
        wst["tok_sm"][buf] = sm_toks[1]
        wst["last_sm"] = sm_toks[1]
        if mat == "d":
            rr = scr.rearrange("(c i p two) d -> p c i two d", c=8, i=8, p=P, two=2)
        else:
            rr = scr.rearrange("(i p two) n -> p i two n", i=8, p=P, two=2)
        for koff in (0, 1):
            stg = d_wt if koff == 0 else d_wt_hi
            ev_toks = []
            last_mm = None
            for bg in range(2):
                b = 2 * koff + bg
                for j4 in range(4):
                    jj = koff * 8 + bg * 4 + j4
                    t_mm = pr.emit("tensor", lambda t, jj=jj, b=b, j4=j4: t.matmul(
                        psA[b][:, j4 * P:(j4 + 1) * P], d_wn[:, jj * P:(jj + 1) * P],
                        ident[:], start=True, stop=True),
                        waits=[sm_toks[koff], T_ID] + ([bk.gate(b)] if j4 == 0 else []),
                        inc="pe", tag=f"WT{ui}_{jj}")
                    last_mm = t_mm
                t_ev = pr.emit("scalar", lambda s, bg=bg, b=b, stg=stg: s.copy(
                    stg[:, bg * 512:(bg + 1) * 512], psA[b][:, 0:512]),
                    waits=[t_mm, wst["tok_store"][koff] if bg == 0 else None], inc="s",
                    tag=f"WTe{ui}_{koff}_{bg}")
                bk.done(b, t_ev)
                ev_toks.append(t_ev)
            wst["tok_wn"][koff] = last_mm
            if mat == "d":
                dst3 = rr[:, cc, :, koff, rt * P:(rt + 1) * P]
            else:
                dst3 = rr[:, :, koff, rt * P:(rt + 1) * P]
            t_st = pr.dma("scalar", dst3, stg[:, :].rearrange("p (i c) -> p i c", i=8),
                          waits=[F(t) for t in ev_toks], tag=f"Wst{ui}_{koff}",
                          stream=f"wst{koff}")
            wst["tok_store"][koff] = t_st
            wst["last_store"] = t_st
        wst["last_mm"] = wst["tok_wn"][1]
        return True

    def pump(n):
        for _ in range(n):
            if not emit_w_unit():
                return

    def drain_w():
        while emit_w_unit():
            pass

    if "A" not in phases:
        drain_w()
    T_W_V = None
    T_W_REL = []

    for nm in ("scrg", "scrd"):
        ap = dbg_out(nm, [D, 512] if nm == "scrg" else [512, D], BF16)
        if ap is not None:
            drain_w()
            pr.dma("sync", ap, scr_g[:, 0:512] if nm == "scrg" else scr_d[0:512, :],
                   waits=[wst["last_store"]], tag="dbg" + nm)
    ap = dbg_out("scrd0", [NF, P], BF16)
    if ap is not None:
        drain_w()
        pr.dma("sync", ap, scr_d[:, 0:P], waits=[wst["last_store"]], tag="dbgscrd0")

    def emit_ln(x_ap, out_ap, waits, tag):
        t_sum = pr.emit("vector", lambda v: v.reduce_sum(stats[:, 0:1], x_ap, axis=AX.X),
                        waits=waits, inc="v", tag=tag + "sum")
        t_sq = pr.emit("scalar", lambda s: s.activation(xnnat[:], x_ap, AF.Square,
                                                        accum_out=stats[:, 1:2]),
                       waits=waits, inc="s", tag=tag + "sq")
        t_mean = pr.emit("vector", lambda v: v.tensor_scalar(
            stats[:, 2:3], stats[:, 0:1], 1.0 / D, None, ALU.mult), waits=[F(t_sum)], inc="v")
        t_e2 = pr.emit("vector", lambda v: v.tensor_scalar(
            stats[:, 3:4], stats[:, 1:2], 1.0 / D, None, ALU.mult), waits=[t_sq], inc="v")
        t_var = pr.emit("vector", lambda v: v.scalar_tensor_tensor(
            stats[:, 4:5], stats[:, 2:3], stats[:, 2:3], stats[:, 3:4], ALU.mult, ALU.subtract),
            waits=[F(t_mean), F(t_e2)], inc="v", tag=tag + "var")
        t_sd = pr.emit("scalar", lambda s: s.activation(stats[:, 5:6], stats[:, 4:5], AF.Sqrt,
                                                        bias=eps_t[:], scale=-1.0),
                       waits=[t_var], inc="s")
        t_rstd = pr.emit("vector", lambda v: v.reciprocal(stats[:, 6:7], stats[:, 5:6]),
                         waits=[t_sd], inc="v")
        t_b = pr.emit("vector", lambda v: v.tensor_scalar(
            stats[:, 7:8], stats[:, 2:3], stats[:, 6:7], -1.0, ALU.mult, ALU.mult),
            waits=[F(t_rstd)], inc="v")
        t_xn = pr.emit("vector", lambda v: v.tensor_scalar(
            out_ap, x_ap, stats[:, 6:7], stats[:, 7:8], ALU.mult, ALU.add),
            waits=[F(t_b), t_sq], inc="v", tag=tag + "xn")
        return t_xn, t_sd

    tok_stats = None
    T_X2 = None
    T_AOT = None
    if "A" in phases:
        drain_w()
        tok_xtile = [None, None]
        tok_xnnat = None
        for tt in range(TTILES):
            buf = tt % 2
            xt = xtile[:, buf * D:(buf + 1) * D]
            src = d_in["x_halo"] if tt == 0 else d_in["x_loc"][(tt - 1) * P:tt * P, :]
            t_ld = pr.dma("sync", xt, src, waits=[tok_xtile[buf], T_W_V], tag=f"A1ld{tt}", stream=f"a1x{buf}")
            t_xn, t_sd = emit_ln(xt, xnnat[:], [t_ld, tok_xnnat, tok_stats, T_W_V], f"A1_{tt}")
            tok_stats = t_sd
            tok_xtile[buf] = t_xn
            for q4 in range(4):
                b = 4 + q4 % 2
                for k4 in range(4):
                    i = q4 * 4 + k4
                    t_mm = pr.emit("tensor", lambda t, i=i, b=b, k4=k4: t.matmul(
                        psA[b][:, k4 * P:(k4 + 1) * P], xnnat[:, i * P:(i + 1) * P], ident[:],
                        start=True, stop=True),
                        waits=[t_xn, T_ID] + ([bk.gate(b)] if k4 == 0 else []),
                        inc="pe", tag=f"A1T{tt}_{q4}_{k4}")
                def eva1(v, q4=q4, b=b, tt=tt):
                    d3 = xnT[:, q4 * 4 * TT:(q4 * 4 + 4) * TT] \
                        .rearrange("p (i r) -> p i r", i=4)[:, :, tt * P:(tt + 1) * P]
                    return v.tensor_copy(d3, psA[b][:, 0:512].rearrange("p (i r) -> p i r", i=4))
                t_ev = pr.emit("vector", eva1, waits=[t_mm], inc="v", tag=f"A1E{tt}_{q4}")
                bk.done(b, t_ev)
            tok_xnnat = t_mm
            pump(1)
        TOK_XNT = pr.marker("vector")

        ap = dbg_out("xnT", [P, DT * TT], BF16)
        if ap is not None:
            pr.dma("sync", ap, xnT, waits=[TOK_XNT], tag="dbgxnT")

        tok_wrow = [None, None]
        tok_wrb = [None, None]
        tok_qkv = {"q": None, "k": None, "v": None}
        tok_expb = [None, None]
        tok_rbc = None
        t_ao = None

        for g in range(G):
            for which, wsrc, dstbuf in (("q", d_in["wq"], qT), ("k", d_in["wk"], kT)):
                for pt in range(2):
                    buf = pt % 2
                    wr = wrow[:, buf * D:(buf + 1) * D]
                    wrb = wrow_b[:, buf * D:(buf + 1) * D]
                    r0 = DPG * g + P * pt
                    t_ld = pr.dma("sync", wr, wsrc[r0:r0 + P, :], waits=[tok_wrow[buf]],
                                  tag=f"A2ld{which}{g}{pt}", stream=f"wr{buf}")
                    t_cast = pr.emit("scalar", lambda s, wr=wr, wrb=wrb: s.copy(wrb, wr),
                                     waits=[t_ld, tok_wrb[buf]], inc="s")
                    tok_wrow[buf] = t_cast
                    ev_toks = []
                    for q4 in range(4):
                        b = q4 % 2
                        for k4 in range(4):
                            i = q4 * 4 + k4
                            t_mm = pr.emit("tensor", lambda t, wrb=wrb, i=i, b=b, k4=k4: t.matmul(
                                psA[b][:, k4 * P:(k4 + 1) * P], wrb[:, i * P:(i + 1) * P], ident[:],
                                start=True, stop=True),
                                waits=[t_cast, T_ID] + ([bk.gate(b)] if k4 == 0 else []),
                                inc="pe", tag=f"A2T{which}{g}{pt}_{i}")
                        t_ev = pr.emit("vector", lambda v, q4=q4, b=b: v.tensor_copy(
                            wT_a[:, q4 * 512:(q4 + 1) * 512], psA[b][:, 0:512]),
                            waits=[t_mm], inc="v")
                        bk.done(b, t_ev)
                        ev_toks.append(t_ev)
                    tok_wrb[buf] = t_mm
                    if g == G - 1 and which == "q" and pt == 0:
                        ap = dbg_out("wTa", [P, DT * P], BF16)
                        if ap is not None:
                            pr.dma("sync", ap, wT_a, waits=[ev_toks[-1]], tag="dbgwTa")
                    last_c = {}
                    for i in range(DT):
                        for ci, (c0, cw) in enumerate(TCH):
                            b = 2 + ci
                            t_mm = pr.emit("tensor", lambda t, i=i, b=b, c0=c0, cw=cw: t.matmul(
                                psA[b][:, 0:cw], wT_a[:, i * P:(i + 1) * P],
                                xnT[:, i * TT + c0: i * TT + c0 + cw],
                                start=(i == 0), stop=(i == DT - 1)),
                                waits=([ev_toks[-1], TOK_XNT, bk.gate(b)] if i == 0 else []),
                                inc="pe", tag=f"A2M{which}{g}{pt}_{c0}_{i}")
                            last_c[ci] = t_mm
                    for ci, (c0, cw) in enumerate(TCH):
                        b = 2 + ci
                        t_ev = pr.emit("vector", lambda v, b=b, pt=pt, c0=c0, cw=cw, dstbuf=dstbuf:
                                       v.tensor_copy(dstbuf[:, pt * TT + c0: pt * TT + c0 + cw],
                                                     psA[b][:, 0:cw]),
                                       waits=[last_c[ci]], inc="v", tag=f"A2E{which}{g}{pt}_{c0}")
                        bk.done(b, t_ev)
                    pump(3)
                tok_qkv[which] = t_ev
            ev_toks_v = []
            for pt in range(2):
                buf = pt % 2
                wr = wrow[:, buf * D:(buf + 1) * D]
                wrb = wrow_b[:, buf * D:(buf + 1) * D]
                r0 = DPG * g + P * pt
                t_ld = pr.dma("sync", wr, d_in["wv"][r0:r0 + P, :], waits=[tok_wrow[buf]],
                              tag=f"A2ldv{g}{pt}", stream=f"wr{buf}")
                t_cast = pr.emit("scalar", lambda s, wr=wr, wrb=wrb: s.copy(wrb, wr),
                                 waits=[t_ld, tok_wrb[buf]], inc="s")
                tok_wrow[buf] = t_cast
                for q4 in range(4):
                    b = q4 % 2
                    for k4 in range(4):
                        i = q4 * 4 + k4
                        t_mm = pr.emit("tensor", lambda t, wrb=wrb, i=i, b=b, k4=k4: t.matmul(
                            psA[b][:, k4 * P:(k4 + 1) * P], wrb[:, i * P:(i + 1) * P], ident[:],
                            start=True, stop=True),
                            waits=[t_cast, T_ID] + ([bk.gate(b)] if k4 == 0 else []),
                            inc="pe", tag=f"A2Tv{g}{pt}_{i}")
                    def evv(v, q4=q4, pt=pt, b=b):
                        d3 = wT_v[:, q4 * 4 * DPG:(q4 * 4 + 4) * DPG] \
                            .rearrange("p (i r) -> p i r", i=4)[:, :, pt * P:(pt + 1) * P]
                        return v.tensor_copy(d3, psA[b][:, 0:512].rearrange("p (i r) -> p i r", i=4))
                    t_ev = pr.emit("vector", evv, waits=[t_mm], inc="v")
                    bk.done(b, t_ev)
                    ev_toks_v.append(t_ev)
                tok_wrb[buf] = t_mm
            for tt in range(TTILES):
                b = 2 + tt % 4
                for i in range(DT):
                    t_mm = pr.emit("tensor", lambda t, i=i, b=b, tt=tt: t.matmul(
                        psA[b][:, 0:DPG], xnT[:, i * TT + tt * P: i * TT + (tt + 1) * P],
                        wT_v[:, i * DPG:(i + 1) * DPG],
                        start=(i == 0), stop=(i == DT - 1)),
                        waits=([ev_toks_v[-1], TOK_XNT, bk.gate(b)] if i == 0 else []),
                        inc="pe", tag=f"A2Mv{g}{tt}_{i}")
                t_ev = pr.emit("vector", lambda v, b=b, tt=tt: v.tensor_copy(
                    vN[:, tt * DPG:(tt + 1) * DPG], psA[b][:, 0:DPG]),
                    waits=[t_mm], inc="v", tag=f"A2Ev{g}{tt}")
                bk.done(b, t_ev)
                pump(1)
            tok_qkv["v"] = t_ev
            T_QKV = [tok_qkv["q"], tok_qkv["k"], tok_qkv["v"]]

            if g == G - 1:
                for nm, bufp, sz in (("qT", qT, 2 * TT), ("kT", kT, 2 * TT),
                                     ("vN", vN, TTILES * DPG)):
                    ap = dbg_out(nm, [P, sz], BF16)
                    if ap is not None:
                        pr.dma("sync", ap, bufp, waits=T_QKV, tag="dbg" + nm)

            if g == 5 and "W" in phases:
                drain_w()
                T_W_REL = [wst["last_store"], wst["last_mm"], wst["last_sm"]]
            for ib in range(NB):
                msel = 0 if ib == 0 else 1
                t_den_mms = []
                t_av = {}
                for hp in range(HG):
                    base = DK * (hp % 2)
                    ptile = hp // 2
                    sbank = 4 + hp % 2
                    slot = hp % 2
                    exp_ap = expb[:, slot * 2 * P:(slot + 1) * 2 * P]
                    for cj in range(2):
                        def mm_sc(t, cj=cj, base=base, ptile=ptile, sbank=sbank, ib=ib):
                            return t.matmul(
                                psA[sbank][:, cj * P:(cj + 1) * P],
                                kT[base:base + DK,
                                   ptile * TT + (ib + cj) * P: ptile * TT + (ib + cj + 1) * P],
                                qT[base:base + DK,
                                   ptile * TT + (ib + 1) * P: ptile * TT + (ib + 2) * P],
                                start=True, stop=True, tile_position=(base, 0))
                        mm2 = pr.emit("tensor", mm_sc,
                                      waits=[bk.gate(sbank)] + (T_QKV if (ib == 0 and hp < 2) else []),
                                      inc="pe", tag=f"A3S{g}_{ib}_{hp}_{cj}")
                    t_exp = pr.emit("scalar", lambda s, exp_ap=exp_ap, sbank=sbank: s.activation(
                        exp_ap, psA[sbank][:, 0:2 * P], AF.Exp, bias=c20[:], scale=0.125),
                        waits=[mm2, T_C20, tok_expb[slot]], inc="s", tag=f"A3X{g}_{ib}_{hp}")
                    bk.done(sbank, t_exp)
                    t_msk = pr.emit("vector", lambda v, exp_ap=exp_ap, msel=msel: v.tensor_tensor(
                        exp_ap, exp_ap, maskTs[:, msel * 2 * P:(msel + 1) * 2 * P], ALU.mult),
                        waits=[t_exp, T_MSK0, T_MSK1], inc="v", tag=f"A3M{g}_{ib}_{hp}")
                    for cj in range(2):
                        t_den = pr.emit("tensor", lambda t, cj=cj, hp=hp, exp_ap=exp_ap: t.matmul(
                            psA[6][0:1, hp * P:(hp + 1) * P], ones_col[:],
                            exp_ap[:, cj * P:(cj + 1) * P],
                            start=(cj == 0), stop=(cj == 1)),
                            waits=[t_msk] + ([bk.gate(6)] if (cj == 0 and hp == 0) else []),
                            inc="pe", tag=f"A3D{g}_{ib}_{hp}_{cj}")
                    t_den_mms.append(t_den)
                    vbank = 7 if hp < 2 else 3
                    for cj in range(2):
                        def mm_av(t, cj=cj, hp=hp, base=base, ib=ib, vbank=vbank, exp_ap=exp_ap):
                            return t.matmul(
                                psA[vbank][base:base + DK, 0:P],
                                vN[:, (ib + cj) * DPG + hp * DK: (ib + cj) * DPG + (hp + 1) * DK],
                                exp_ap[:, cj * P:(cj + 1) * P],
                                start=(cj == 0), stop=(cj == 1),
                                tile_position=(0, base))
                        t_avm = pr.emit("tensor", mm_av,
                                        waits=[t_msk] + ([bk.gate(vbank)] if (cj == 0 and hp % 2 == 0) else []),
                                        inc="pe", tag=f"A3V{g}_{ib}_{hp}_{cj}")
                    t_av[hp] = t_avm
                    tok_expb[slot] = t_avm
                t_rec = pr.emit("vector", lambda v: v.reciprocal(recip[:], psA[6][0:1, 0:HG * P]),
                                waits=t_den_mms, inc="v", tag=f"A3R{g}_{ib}")
                bk.done(6, t_rec)
                for p2 in range(2):
                    vbank = 7 if p2 == 0 else 3
                    for par in range(2):
                        hp = 2 * p2 + par
                        def mm_bc(t, par=par, hp=hp, vbank=vbank):
                            return t.matmul(
                                psA[vbank][par * DK:(par + 1) * DK, P:2 * P],
                                ones_r64[:], recip[0:1, hp * P:(hp + 1) * P],
                                start=True, stop=True, tile_position=(0, par * DK))
                        t_bc = pr.emit("tensor", mm_bc, waits=[t_rec], inc="pe",
                                       tag=f"A3B{g}_{ib}_{hp}")
                    rb = rbc[:, p2 * P:(p2 + 1) * P]
                    t_rbc = pr.emit("vector", lambda v, vbank=vbank, rb=rb: v.tensor_copy(
                        rb, psA[vbank][:, P:2 * P]),
                        waits=[t_bc, tok_rbc], inc="v")
                    dto = 2 * g + p2
                    t_ao = pr.emit("vector", lambda v, rb=rb, vbank=vbank, dto=dto, ib=ib:
                                   v.tensor_tensor(
                                       aoT[:, dto * TLOC + ib * P: dto * TLOC + (ib + 1) * P],
                                       psA[vbank][:, 0:P], rb, ALU.mult),
                                   waits=[t_av[2 * p2], t_av[2 * p2 + 1], F(t_rbc)] +
                                         (T_W_REL if dto >= 11 else []), inc="v",
                                   tag=f"A3O{g}_{ib}_{p2}")
                    bk.done(vbank, t_ao)
                    tok_rbc = t_ao
                pump(3)

        T_AOT = pr.marker("vector")
        ap = dbg_out("aoT", [P, DT * TLOC], BF16)
        if ap is not None:
            pr.dma("sync", ap, aoT, waits=[T_AOT], tag="dbgaoT")

        tok_projb = None
        tok_xrld = None
        last_add = None
        for dj in range(DT):
            buf = dj % 2
            wr = wrow[:, buf * D:(buf + 1) * D]
            wrb = wrow_b[:, buf * D:(buf + 1) * D]
            t_ld = pr.dma("sync", wr, d_in["wo"][dj * P:(dj + 1) * P, :],
                          waits=[tok_wrow[buf]], tag=f"A4ld{dj}", stream=f"wr{buf}")
            t_cast = pr.emit("scalar", lambda s, wr=wr, wrb=wrb: s.copy(wrb, wr),
                             waits=[t_ld, tok_wrb[buf]], inc="s")
            tok_wrow[buf] = t_cast
            ev_toks = []
            for m in range(DT):
                b = m % 2
                t_mm = pr.emit("tensor", lambda t, wrb=wrb, m=m, b=b: t.matmul(
                    psA[b][:, 0:P], wrb[:, m * P:(m + 1) * P], ident[:],
                    start=True, stop=True),
                    waits=[t_cast, T_ID, bk.gate(b)], inc="pe", tag=f"A4T{dj}_{m}")
                t_ev = pr.emit("vector", lambda v, m=m, b=b: v.tensor_copy(
                    wT_a[:, m * P:(m + 1) * P], psA[b][:, 0:P]),
                    waits=[t_mm], inc="v")
                bk.done(b, t_ev)
                ev_toks.append(t_ev)
            tok_wrb[buf] = t_mm
            for tc in range(4):
                b = 2
                for m in range(DT):
                    t_mm = pr.emit("tensor", lambda t, m=m, tc=tc, b=b: t.matmul(
                        psA[b][:, 0:512], wT_a[:, m * P:(m + 1) * P],
                        aoT[:, m * TLOC + tc * 512: m * TLOC + (tc + 1) * 512],
                        start=(m == 0), stop=(m == DT - 1)),
                        waits=([ev_toks[-1], T_AOT, bk.gate(b)] if m == 0 else []),
                        inc="pe", tag=f"A4M{dj}_{tc}_{m}")
                t_pb = pr.emit("scalar", lambda s, b=b: s.copy(projb[:], psA[b][:, 0:512]),
                               waits=[t_mm, tok_projb], inc="s", tag=f"A4P{dj}_{tc}")
                bk.done(b, t_pb)
                t_xr = pr.dma("sync", xrld[:],
                              d_in["x_loc"][tc * 512:(tc + 1) * 512, dj * P:(dj + 1) * P]
                              .rearrange("(a p) b -> p a b", p=P),
                              waits=[tok_xrld], tag=f"A4xr{dj}_{tc}", stream="xr")
                for s4 in range(4):
                    b2 = 4 + s4
                    t_mm2 = pr.emit("tensor", lambda t, s4=s4, b2=b2: t.matmul(
                        psA[b2][:, 0:P], projb[:, s4 * P:(s4 + 1) * P], ident[:],
                        start=True, stop=True),
                        waits=[t_pb, bk.gate(b2)], inc="pe", tag=f"A4N{dj}_{tc}_{s4}")
                    tti = tc * 4 + s4
                    t_add = pr.emit("vector", lambda v, s4=s4, b2=b2, tti=tti, dj=dj:
                                    v.tensor_tensor(
                                        x2[:, tti * D + dj * P: tti * D + (dj + 1) * P],
                                        psA[b2][:, 0:P], xrld[:, s4 * P:(s4 + 1) * P], ALU.add),
                                    waits=[t_mm2, t_xr], inc="v",
                                    tag=f"A4A{dj}_{tc}_{s4}")
                    bk.done(b2, t_add)
                    last_add = t_add
                tok_projb = t_mm2
                tok_xrld = last_add
        T_X2 = pr.marker("vector")
        ap = dbg_out("x2", [P, NB * D], BF16)
        if ap is not None:
            pr.dma("sync", ap, x2, waits=[T_X2], tag="dbgx2")

    if "W" in phases:
        drain_w()
        T_W_DONE = [wst["last_store"]]
        for nm, scr_, shp in (("scrgZ", scr_g, [D, NF]), ("scruZ", scr_u, [D, NF]),
                              ("scrdZ", scr_d, [NF, D])):
            ap = dbg_out(nm, shp, BF16)
            if ap is not None:
                pr.dma("sync", ap, scr_[:, :], waits=T_W_DONE, tag="dbg" + nm)
    else:
        T_W_DONE = []
    if "B" in phases:
        st_toks = []
        tok_wgt = [None, None]
        tok_wut = [None, None]
        tok_wdt = [None, None]
        tok_g1 = None
        tok_xn2 = None
        tok_otile = None
        for slab in range(2):
            tok_acc = {}
            for u in range(8):
                tti = slab * 8 + u
                x2t = x2[:, tti * D:(tti + 1) * D]
                t_xn, t_sd = emit_ln(x2t, xnnat[:], [T_X2, tok_xn2, tok_stats], f"B1_{slab}_{u}")
                tok_stats = t_sd
                for q4 in range(4):
                    b = q4 % 2
                    for k4 in range(4):
                        i = q4 * 4 + k4
                        t_mm = pr.emit("tensor", lambda t, i=i, b=b, k4=k4: t.matmul(
                            psA[b][:, k4 * P:(k4 + 1) * P], xnnat[:, i * P:(i + 1) * P], ident[:],
                            start=True, stop=True),
                            waits=[t_xn, T_ID] + ([bk.gate(b)] if k4 == 0 else []),
                            inc="pe", tag=f"B1T{slab}_{u}_{q4}_{k4}")
                    def evx(v, q4=q4, b=b, u=u):
                        d3 = xn2T[:, q4 * 4 * SLAB:(q4 * 4 + 4) * SLAB] \
                            .rearrange("p (i r) -> p i r", i=4)[:, :, u * P:(u + 1) * P]
                        return v.tensor_copy(d3, psA[b][:, 0:512].rearrange("p (i r) -> p i r", i=4))
                    t_ev = pr.emit("vector", evx, waits=[t_mm], inc="v")
                    bk.done(b, t_ev)
                tok_xn2 = t_mm
            TOK_XN2 = pr.marker("vector")
            for nm in (f"acc{slab}i", f"xn2{slab}"):
                ap = dbg_out(nm, [P, ACCOFF], BF16)
                if ap is not None:
                    pr.dma("sync", ap, aoT[:, 0:ACCOFF] if nm.startswith("acc") else xn2T,
                           waits=[TOK_XN2], tag="dbg" + nm)

            for sup in range(8):
                inter_toks = []
                for ch in range(16):
                    c0 = sup * 2048 + ch * P
                    buf = ch % 2
                    wg = wgt[:, buf * 2048:(buf + 1) * 2048]
                    wu = wut[:, buf * 2048:(buf + 1) * 2048]
                    t_wg = pr.dma("sync", wg,
                                  scr_g[:, c0:c0 + P].rearrange("(i p) n -> p i n", p=P),
                                  waits=[tok_wgt[buf], T_X2] + T_W_DONE,
                                  tag=f"B2g{slab}_{sup}_{ch}", stream=f"bg{buf}")
                    t_wu = pr.dma("gpsimd", wu,
                                  scr_u[:, c0:c0 + P].rearrange("(i p) n -> p i n", p=P),
                                  waits=[tok_wut[buf], T_X2] + T_W_DONE,
                                  tag=f"B2u{slab}_{sup}_{ch}", stream=f"bu{buf}")
                    ntl = ch
                    lastg = {}
                    lastu = {}
                    for i in range(DT):
                        for tc in range(2):
                            t_mmg = pr.emit("tensor", lambda t, i=i, tc=tc, wg=wg: t.matmul(
                                psA[tc][:, 0:512], wg[:, i * P:(i + 1) * P],
                                xn2T[:, i * SLAB + tc * 512: i * SLAB + (tc + 1) * 512],
                                start=(i == 0), stop=(i == DT - 1)),
                                waits=([t_wg, TOK_XN2, bk.gate(tc)] if i == 0 else []),
                                inc="pe", tag=f"B2G{slab}_{sup}_{ntl}_{tc}_{i}")
                            lastg[tc] = t_mmg
                    for i in range(DT):
                        for tc in range(2):
                            t_mmu = pr.emit("tensor", lambda t, i=i, tc=tc, wu=wu: t.matmul(
                                psA[2 + tc][:, 0:512], wu[:, i * P:(i + 1) * P],
                                xn2T[:, i * SLAB + tc * 512: i * SLAB + (tc + 1) * 512],
                                start=(i == 0), stop=(i == DT - 1)),
                                waits=([t_wu, TOK_XN2, bk.gate(2 + tc)] if i == 0 else []),
                                inc="pe", tag=f"B2U{slab}_{sup}_{ntl}_{tc}_{i}")
                            lastu[tc] = t_mmu
                    for tc in range(2):
                        t_gelu = pr.emit("scalar", lambda s, tc=tc: s.activation(
                            g1[:], psA[tc][:, 0:512], AF.Gelu_apprx_tanh),
                            waits=[lastg[tc], tok_g1], inc="s", tag=f"B2A{slab}_{sup}_{ntl}_{tc}")
                        bk.done(tc, t_gelu)
                        t_int = pr.emit("vector", lambda v, ntl=ntl, tc=tc: v.tensor_tensor(
                            aoT[:, ACCOFF + ntl * SLAB + tc * 512: ACCOFF + ntl * SLAB + (tc + 1) * 512],
                            psA[2 + tc][:, 0:512], g1[:], ALU.mult),
                            waits=[lastu[tc], t_gelu], inc="v", tag=f"B2I{slab}_{sup}_{ntl}_{tc}")
                        bk.done(2 + tc, t_int)
                        tok_g1 = t_int
                        inter_toks.append(t_int)
                    tok_wgt[buf] = lastg[1]
                    tok_wut[buf] = lastu[1]
                for ch in range(16):
                    buf = ch % 2
                    wd = wdt[:, buf * 2048:(buf + 1) * 2048]
                    t_wd = pr.dma("gpsimd", wd,
                                  scr_d[sup * 2048:(sup + 1) * 2048, ch * P:(ch + 1) * P]
                                  .rearrange("(i p) n -> p i n", p=P),
                                  waits=[tok_wdt[buf], T_X2] + T_W_DONE,
                                  tag=f"B3d{slab}_{sup}_{ch}", stream=f"bd{buf}")
                    hti = ch
                    lastd = {}
                    for i in range(DT):
                        for tc in range(2):
                            b = 4 + 2 * (ch % 2) + tc
                            t_mmd = pr.emit("tensor", lambda t, i=i, tc=tc, b=b, wd=wd: t.matmul(
                                psA[b][:, 0:512], wd[:, i * P:(i + 1) * P],
                                aoT[:, ACCOFF + i * SLAB + tc * 512: ACCOFF + i * SLAB + (tc + 1) * 512],
                                start=(i == 0), stop=(i == DT - 1)),
                                waits=([t_wd, bk.gate(b)] + inter_toks if i == 0 else []),
                                inc="pe", tag=f"B3M{slab}_{sup}_{hti}_{tc}_{i}")
                            lastd[tc] = t_mmd
                    if slab == 1 and sup == 0 and ch == 0:
                        ap = dbg_out("dpsum", [P, 512], F32)
                        if ap is not None:
                            t_cp = pr.emit("vector", lambda v: v.tensor_copy(
                                otile[:, 0:512], psA[4][:, 0:512]),
                                waits=[lastd[0], lastd[1]], inc="v", tag="dbgdpsumcp")
                            pr.dma("sync", ap, otile[:, 0:512], waits=[t_cp], tag="dbgdpsum")
                        ap = dbg_out("dwdt", [P, DT * P], BF16)
                        if ap is not None:
                            t_cpw = pr.emit("vector", lambda v, wd=wd: v.tensor_copy(
                                xnnat[:], wd), waits=[t_wd], inc="v", tag="dbgdwdtcp")
                            pr.dma("sync", ap, xnnat[:], waits=[t_cpw], tag="dbgdwdt")
                    for tc in range(2):
                        b = 4 + 2 * (ch % 2) + tc
                        key = (hti, tc)
                        if sup == 0:
                            t_acc = pr.emit("vector", lambda v, hti=hti, tc=tc, b=b: v.tensor_copy(
                                aoT[:, hti * SLAB + tc * 512: hti * SLAB + (tc + 1) * 512],
                                psA[b][:, 0:512]),
                                waits=[lastd[tc], tok_acc.get(key)], inc="v",
                                tag=f"B3A{slab}_{sup}_{hti}_{tc}")
                        else:
                            t_acc = pr.emit("vector", lambda v, hti=hti, tc=tc, b=b: v.tensor_tensor(
                                aoT[:, hti * SLAB + tc * 512: hti * SLAB + (tc + 1) * 512],
                                psA[b][:, 0:512],
                                aoT[:, hti * SLAB + tc * 512: hti * SLAB + (tc + 1) * 512],
                                ALU.add),
                                waits=[lastd[tc], tok_acc.get(key)], inc="v",
                                tag=f"B3A{slab}_{sup}_{hti}_{tc}")
                        bk.done(b, t_acc)
                        tok_acc[key] = t_acc
                    tok_wdt[buf] = lastd[1]
                for nm, rg0, rg1 in ((f"accS{slab}_{sup}", 0, ACCOFF),
                                     (f"intS{slab}_{sup}", ACCOFF, 2 * ACCOFF)):
                    ap = dbg_out(nm, [P, ACCOFF], BF16)
                    if ap is not None:
                        toks = [tok_acc[kk2] for kk2 in tok_acc]
                        pr.dma("sync", ap, aoT[:, rg0:rg1], waits=toks, tag="dbg" + nm)
            for u in range(8):
                for half in range(2):
                    ev_last = None
                    for q4 in range(2):
                        b = q4 % 2
                        for k4 in range(4):
                            i = half * 8 + q4 * 4 + k4
                            t_mm = pr.emit("tensor", lambda t, i=i, u=u, b=b, k4=k4: t.matmul(
                                psA[b][:, k4 * P:(k4 + 1) * P],
                                aoT[:, i * SLAB + u * P: i * SLAB + (u + 1) * P],
                                ident[:], start=True, stop=True),
                                waits=[tok_acc.get((i, u // 4)), T_ID] +
                                      ([bk.gate(b), tok_otile] if k4 == 0 else []),
                                inc="pe", tag=f"B4T{slab}_{u}_{i}")
                        tti = slab * 8 + u
                        x2sl = x2[:, tti * D + half * 1024 + q4 * 512:
                                  tti * D + half * 1024 + (q4 + 1) * 512]
                        t_ev = pr.emit("vector", lambda v, q4=q4, b=b, x2sl=x2sl: v.tensor_tensor(
                            otile[:, q4 * 512:(q4 + 1) * 512], psA[b][:, 0:512], x2sl, ALU.add),
                            waits=[t_mm], inc="v")
                        bk.done(b, t_ev)
                        ev_last = t_ev
                    t_st = pr.dma("sync",
                                  out_t[slab * SLAB + u * P: slab * SLAB + (u + 1) * P,
                                        half * 1024:(half + 1) * 1024],
                                  otile[:], waits=[ev_last], tag=f"B4S{slab}_{u}_{half}",
                                  stream="bst")
                    st_toks.append(t_st)
                    tok_otile = t_st
        pr.emit("gpsimd", lambda gq: gq.engine_nop(), waits=[st_toks[-1]], inc="g", tag="final")
    else:
        allw = [(s, pr.cnt[s], "dma") for s in pr.cnt if s.startswith("dma_") and pr.cnt[s]]
        pr.emit("gpsimd", lambda gq: gq.engine_nop(), waits=allw, inc="g", tag="final")

    return nc, ctx, pr, d_in, out_t, dbg


def finish(nc, ctx, pr):
    block = ctx.enter_context(nc.Block())
    pr.build(block)
    ctx.close()
    return nc


from concourse.bass_utils import run_bass_kernel_spmd
import ml_dtypes

NCORES = 8

_mask_cache = {}


def _masks():
    if "m" not in _mask_cache:
        r = np.arange(P)[None, :]
        c = np.arange(2 * P)[:, None]
        rest = ((c >= r) & (c <= r + P)).astype(np.float32)
        first = rest * (c >= P)
        _mask_cache["m"] = (first.astype(ml_dtypes.bfloat16),
                           rest.astype(ml_dtypes.bfloat16))
    return _mask_cache["m"]


_prog_cache = {}


def _get_nc():
    if "nc" not in _prog_cache:
        nc, ctx, pr, d_in, out_t, dbg = build_program(phases=("W", "A", "B"), debug=())
        block = ctx.enter_context(nc.Block())
        pr.build(block)
        ctx.close()
        _prog_cache["nc"] = nc
    return _prog_cache["nc"]


def kernel(L0_attn_x, L0_attn_Wq, L0_attn_Wk, L0_attn_Wv, L0_attn_Wo,
           L0_ffn_gate_W_packed, L0_ffn_gate_scales,
           L0_ffn_up_W_packed, L0_ffn_up_scales,
           L0_ffn_down_W_packed, L0_ffn_down_scales):
    x = np.ascontiguousarray(np.asarray(L0_attn_x, dtype=np.float32).reshape(NCORES * TLOC, D))
    wq = np.ascontiguousarray(np.asarray(L0_attn_Wq, dtype=np.float32))
    wk = np.ascontiguousarray(np.asarray(L0_attn_Wk, dtype=np.float32))
    wv = np.ascontiguousarray(np.asarray(L0_attn_Wv, dtype=np.float32))
    wo = np.ascontiguousarray(np.asarray(L0_attn_Wo, dtype=np.float32))
    def _bytes(a):
        a = np.ascontiguousarray(np.asarray(a, dtype=np.int32))
        b = a.view(np.uint8).reshape(a.shape[0], -1)[:, 0::4]
        return np.ascontiguousarray(b.astype(np.uint16))
    gp = _bytes(L0_ffn_gate_W_packed)
    gs = np.ascontiguousarray(np.asarray(L0_ffn_gate_scales, dtype=np.float32))
    up = _bytes(L0_ffn_up_W_packed)
    us = np.ascontiguousarray(np.asarray(L0_ffn_up_scales, dtype=np.float32))
    dp = _bytes(L0_ffn_down_W_packed)
    ds = np.ascontiguousarray(np.asarray(L0_ffn_down_scales, dtype=np.float32))
    mfirst, mrest = _masks()
    zero_halo = np.zeros((HALO, D), dtype=np.float32)

    in_maps = []
    for c in range(NCORES):
        halo = zero_halo if c == 0 else x[c * TLOC - HALO:c * TLOC]
        in_maps.append(dict(
            x_loc=x[c * TLOC:(c + 1) * TLOC],
            x_halo=np.ascontiguousarray(halo),
            wq=wq, wk=wk, wv=wv, wo=wo,
            gate_p=gp, gate_s=gs, up_p=up, up_s=us, down_p=dp, down_s=ds,
            maskT_first=(mfirst if c == 0 else mrest),
            maskT_rest=mrest,
        ))

    nc = _get_nc()
    res = run_bass_kernel_spmd(nc, in_maps, core_ids=list(range(NCORES)))
    out = np.concatenate([np.asarray(r["out"], dtype=np.float32) for r in res.results], axis=0)
    return out.reshape(1, NCORES * TLOC, D)


# version 13
# speedup vs baseline: 1.1877x; 1.0013x over previous
import sys
if "/opt/trn_rl_repo" not in sys.path:
    sys.path.insert(0, "/opt/trn_rl_repo")
import numpy as np
from contextlib import ExitStack

import concourse.bass as bass
import concourse.mybir as mybir

F32 = mybir.dt.float32
BF16 = mybir.dt.bfloat16
I16 = mybir.dt.int16
I32 = mybir.dt.int32
U8 = mybir.dt.uint8
U16 = mybir.dt.uint16
AF = mybir.ActivationFunctionType
ALU = mybir.AluOpType
AX = mybir.AxisListType

P = 128
D = 2048
DT = D // P
TLOC = 2048
HALO = 128
TT = TLOC + HALO
TTILES = TT // P
NB = TLOC // P
H = 32
DK = 64
HG = 4
G = H // HG
DPG = HG * DK
NF = 16384
EXP_OFF = -20.0
SLAB = 1024
ACCOFF = DT * SLAB

TCH = [(0, 512), (512, 512), (1024, 512), (1536, 512), (2048, 128)]


class Prog:
    def __init__(self, nc, ctx):
        self.nc = nc
        self.ctx = ctx
        self.ops = []
        self.sems = {}
        self.cnt = {}

    def sem(self, name):
        if name not in self.sems:
            self.sems[name] = self.ctx.enter_context(self.nc.semaphore(name))
            self.cnt[name] = 0
        return self.sems[name]

    def emit(self, engine, fn, waits=(), inc=None, inc_by=1, tag=""):
        w = []
        for t in waits:
            if t is None:
                continue
            s, v, peng = t
            if peng == engine and not s.startswith("dma"):
                continue
            w.append((s, v))
        wd = {}
        for s, v in w:
            wd[s] = max(wd.get(s, 0), v)
        tok = None
        if inc is not None:
            self.sem(inc)
            self.cnt[inc] += inc_by
            tok = (inc, self.cnt[inc], engine if inc_by == 1 else "dma")
        self.ops.append((engine, fn, sorted(wd.items()), inc, inc_by, tag))
        return tok

    def marker(self, engine="vector"):
        return self.emit(engine, lambda e: e.engine_nop(), inc="mark", tag="marker")

    def dma(self, queue, out, in_, waits=(), tag="", stream=None):
        sname = "dma_" + (stream or queue)
        self.sem(sname)
        w = list(waits)
        if self.cnt[sname]:
            w.append((sname, self.cnt[sname], "dma"))

        def fn(eng):
            return eng.dma_start(out=out, in_=in_)
        return self.emit(queue, fn, w, inc=sname, inc_by=16, tag=tag)

    def build(self, block):
        sems = self.sems

        def make(engine_name):
            def closure(eng):
                for (e, fn, waits, inc, inc_by, tag) in self.ops:
                    if e != engine_name:
                        continue
                    for (s, v) in waits:
                        eng.wait_ge(sems[s], v)
                    inst = fn(eng)
                    if inc is not None:
                        assert inst is not None, f"op {tag} returned no instruction"
                        inst.then_inc(sems[inc], inc_by)
            return closure

        block.sync(make("sync"))
        block.scalar(make("scalar"))
        block.vector(make("vector"))
        block.tensor(make("tensor"))
        block.gpsimd(make("gpsimd"))


def F(tok):
    return None if tok is None else (tok[0], tok[1], "__force__")


class Banks:
    def __init__(self, n=8):
        self.tok = [None] * n

    def gate(self, b):
        return self.tok[b]

    def done(self, b, token):
        self.tok[b] = token


def build_program(phases=("W", "A", "B"), debug=()):
    nc = bass.Bass("TRN2", target_bir_lowering=False, debug=False)
    ctx = ExitStack()

    d_in = {}
    d_in["x_loc"] = nc.dram_tensor("x_loc", [TLOC, D], F32, kind="ExternalInput").ap()
    d_in["x_halo"] = nc.dram_tensor("x_halo", [HALO, D], F32, kind="ExternalInput").ap()
    for w in ("wq", "wk", "wv", "wo"):
        d_in[w] = nc.dram_tensor(w, [D, D], F32, kind="ExternalInput").ap()
    d_in["gate_p"] = nc.dram_tensor("gate_p", [NF, D // 2], U8, kind="ExternalInput").ap()
    d_in["gate_s"] = nc.dram_tensor("gate_s", [NF, D // 32], F32, kind="ExternalInput").ap()
    d_in["up_p"] = nc.dram_tensor("up_p", [NF, D // 2], U8, kind="ExternalInput").ap()
    d_in["up_s"] = nc.dram_tensor("up_s", [NF, D // 32], F32, kind="ExternalInput").ap()
    d_in["down_p"] = nc.dram_tensor("down_p", [D, NF // 2], U8, kind="ExternalInput").ap()
    d_in["down_s"] = nc.dram_tensor("down_s", [D, NF // 32], F32, kind="ExternalInput").ap()
    d_in["maskT_first"] = nc.dram_tensor("maskT_first", [2 * P, P], BF16, kind="ExternalInput").ap()
    d_in["maskT_rest"] = nc.dram_tensor("maskT_rest", [2 * P, P], BF16, kind="ExternalInput").ap()

    out_t = nc.dram_tensor("out", [TLOC, D], F32, kind="ExternalOutput").ap()

    scr_g = nc.dram_tensor("scr_g", [D, NF], BF16, kind="Internal").ap()
    scr_u = nc.dram_tensor("scr_u", [D, NF], BF16, kind="Internal").ap()
    scr_d = nc.dram_tensor("scr_d", [NF, D], BF16, kind="Internal").ap()

    dbg = {}

    def dbg_out(name, shape, dt=F32):
        if name in debug:
            dbg[name] = nc.dram_tensor("dbg_" + name, shape, dt, kind="ExternalOutput").ap()
            return dbg[name]
        return None

    AR1 = ctx.enter_context(nc.sbuf_tensor("AR1", [P, 34816], BF16))
    AR2 = ctx.enter_context(nc.sbuf_tensor("AR2", [P, 32768], BF16))
    AR3 = ctx.enter_context(nc.sbuf_tensor("AR3", [P, 32768], BF16))

    def v1(a, b):
        return AR1[:, a:b]

    def v2(a, b):
        return AR2[:, a:b]

    def v3(a, b):
        return AR3[:, a:b]

    xnT = v1(0, 34816)
    x2 = v1(0, 32768)
    otile = v1(32768, 34816).bitcast(F32)
    aoT = v2(0, 32768)
    W_TAIL = 22912
    d_in8 = v2(22912, 23936).bitcast(U8)
    d_b16 = v2(23936, 24960).bitcast(U16)
    d_t1 = v2(24960, 25984).bitcast(U16)
    d_t2 = v2(25984, 27008).bitcast(U16)
    d_t1h = v2(27008, 28032).bitcast(U16)
    d_t2h = v2(28032, 29056).bitcast(U16)
    d_wn = v2(29056, 31104)
    d_sc = v2(31104, 31360).bitcast(F32)
    d_wt = v2(31360, 32384)
    d_scb = v2(32384, 32512)
    d_sce = v1(32768, 34816)
    xtile = v2(0, 8192).bitcast(F32)
    qT = v3(0, 4352)
    kT = v3(4352, 8704)
    vN = v3(8704, 13056)
    wrow = v3(13056, 21248).bitcast(F32)
    wrow_b = v3(21248, 25344)
    wT_a = v3(25344, 27392)
    wT_v = v3(27392, 31488)
    d_wt_hi = v3(31488, 32512)
    xn2T = v3(0, 16384)
    wgt = v3(16384, 20480)
    wut = v3(20480, 24576)
    wdt = v3(24576, 28672)
    xnnat = v3(28672, 30720)
    g1 = v3(30720, 31232)

    def gsb(name, pdim, fsize, dt):
        return ctx.enter_context(nc.sbuf_tensor(name, [pdim, fsize], dt))

    stats = gsb("stats", P, 16, F32)
    eps_t = gsb("eps_t", P, 1, F32)
    c20 = gsb("c20", P, 1, F32)
    ident = gsb("ident", P, P, BF16)
    identf = gsb("identf", P, P, F32)
    ones_col = gsb("ones_col", P, 1, BF16)
    ones_r64 = gsb("ones_r64", 1, DK, F32)
    maskTs = gsb("maskTs", P, 2 * 2 * P, BF16)
    expb = gsb("expb", P, 2 * 2 * P, BF16)
    recip = gsb("recip", 1, HG * P, F32)
    rbc = gsb("rbc", P, 2 * P, F32)
    projb = gsb("projb", P, 512, BF16)
    xrld = gsb("xrld", P, 4 * P, F32)

    psA = [ctx.enter_context(nc.psum_tensor(f"psA{i}", [P, 512], F32)) for i in range(8)]

    pr = Prog(nc, ctx)
    bk = Banks()

    t00 = pr.emit("gpsimd", lambda g: g.memset(identf[:], 0.0), inc="g")
    t0 = pr.emit("gpsimd", lambda g: g.affine_select(
        out=identf[:], in_=identf[:], compare_op=ALU.not_equal, fill=1.0, base=0,
        pattern=[[-1, P]], channel_multiplier=1), waits=[F(t00)], inc="g")
    T_ID = pr.emit("vector", lambda v: v.tensor_copy(ident[:], identf[:]), waits=[t0], inc="v")
    pr.emit("vector", lambda v: v.memset(ones_col[:], 1.0), inc="v")
    pr.emit("vector", lambda v: v.memset(ones_r64[:], 1.0), inc="v")
    pr.emit("vector", lambda v: v.memset(eps_t[:], 1e-5), inc="v")
    T_C20 = pr.emit("vector", lambda v: v.memset(c20[:], EXP_OFF), inc="v")
    T_MSK0 = pr.dma("sync", maskTs[:, 0:2 * P],
                    d_in["maskT_first"].rearrange("(a p) b -> p a b", p=P), tag="mskf")
    T_MSK1 = pr.dma("sync", maskTs[:, 2 * P:4 * P],
                    d_in["maskT_rest"].rearrange("(a p) b -> p a b", p=P), tag="mskr")

    W_UNITS = []
    if "W" in phases:
        for mat, sp, ss, n_rt, n_cc, scr in (
                ("g", d_in["gate_p"], d_in["gate_s"], NF // P, 1, scr_g),
                ("u", d_in["up_p"], d_in["up_s"], NF // P, 1, scr_u),
                ("d", d_in["down_p"], d_in["down_s"], D // P, 8, scr_d)):
            for rt in range(n_rt):
                for cc in range(n_cc):
                    W_UNITS.append((mat, sp, ss, scr, rt, cc))

    wst = {"ui": 0, "tok_in": [None, None], "tok_widen": [None, None],
           "tok_sc": [None, None], "tok_sm": [None, None], "tok_b16": None,
           "tok_scb": [None, None],
           "tok_wn": [None, None], "tok_store": [None, None],
           "last_store": None, "last_mm": None, "last_sm": None}

    def emit_w_unit():
        ui = wst["ui"]
        if ui >= len(W_UNITS):
            return False
        wst["ui"] += 1
        mat, sp, ss, scr, rt, cc = W_UNITS[ui]
        buf = ui % 2
        inb = d_in8[:, buf * 1024:(buf + 1) * 1024]
        scb = d_sc[:, buf * 64:(buf + 1) * 64]
        rows = slice(rt * P, (rt + 1) * P)
        t_in = pr.dma("sync", inb, sp[rows, cc * 1024:(cc + 1) * 1024],
                      waits=[wst["tok_widen"][buf]], tag=f"Win{ui}", stream=f"wpk{buf}")
        t_sc = pr.dma("sync", scb, ss[rows, cc * 64:(cc + 1) * 64],
                      waits=[wst["tok_scb"][buf]], tag=f"Wsc{ui}", stream=f"wsc{buf}")
        t_wd = pr.emit("vector", lambda v, inb=inb: v.tensor_copy(d_b16[:], inb),
                       waits=[t_in, wst["tok_b16"]], inc="v", tag=f"Wwd{ui}")
        wst["tok_widen"][buf] = t_wd
        scbb = d_scb[:, buf * 64:(buf + 1) * 64]
        scee = d_sce[:, buf * 1024:(buf + 1) * 1024]
        t_scb = pr.emit("scalar", lambda s, scbb=scbb, scb=scb: s.copy(scbb, scb),
                        waits=[t_sc, wst["tok_scb"][buf]], inc="s", tag=f"Wscb{ui}")
        t_sce = pr.emit("scalar", lambda s, scee=scee, scbb=scbb: s.copy(
            scee.rearrange("p (b r) -> p b r", r=16),
            scbb.unsqueeze(2).broadcast_to((P, 64, 16))),
            waits=[F(t_scb), wst["tok_sm"][buf]], inc="s", tag=f"Wsce{ui}")
        wst["tok_scb"][buf] = t_sce
        pr.emit("vector", lambda v: v.tensor_scalar(
            d_t1[:], d_b16[:], 7, 6, ALU.bitwise_and, ALU.logical_shift_left),
            waits=[t_wd], inc="v", tag=f"Wc{ui}")
        pr.emit("vector", lambda v: v.tensor_scalar(
            d_t1h[:], d_b16[:], 112, 2, ALU.bitwise_and, ALU.logical_shift_left), inc="v")
        pr.emit("vector", lambda v: v.tensor_scalar(
            d_t2[:], d_b16[:], 8, 12, ALU.bitwise_and, ALU.logical_shift_left), inc="v")
        t_b16free = pr.emit("vector", lambda v: v.tensor_scalar(
            d_t2h[:], d_b16[:], 128, 8, ALU.bitwise_and, ALU.logical_shift_left), inc="v")
        wst["tok_b16"] = t_b16free
        pr.emit("vector", lambda v: v.tensor_tensor(
            d_t2[:], d_t1[:], d_t2[:], ALU.add), inc="v")
        pr.emit("vector", lambda v: v.tensor_tensor(
            d_t2h[:], d_t1h[:], d_t2h[:], ALU.add), inc="v")
        pr.emit("vector", lambda v: v.tensor_single_scalar(
            d_t1[:], d_t1[:], 6, ALU.logical_shift_right), inc="v")
        pr.emit("vector", lambda v: v.tensor_single_scalar(
            d_t1h[:], d_t1h[:], 6, ALU.logical_shift_right), inc="v")
        pr.emit("vector", lambda v: v.tensor_scalar(
            d_t1[:], d_t1[:], 251, 252, ALU.mult, ALU.min), inc="v")
        pr.emit("vector", lambda v: v.tensor_scalar(
            d_t1h[:], d_t1h[:], 251, 252, ALU.mult, ALU.min), inc="v")
        pr.emit("vector", lambda v: v.tensor_single_scalar(
            d_t1[:], d_t1[:], 6, ALU.logical_shift_left), inc="v")
        pr.emit("vector", lambda v: v.tensor_single_scalar(
            d_t1h[:], d_t1h[:], 6, ALU.logical_shift_left), inc="v")
        t_el = pr.emit("vector", lambda v: v.tensor_tensor(
            d_t2[:], d_t1[:], d_t2[:], ALU.add), inc="v")
        t_eh = pr.emit("vector", lambda v: v.tensor_tensor(
            d_t2h[:], d_t1h[:], d_t2h[:], ALU.add), inc="v")
        sm_toks = []
        for koff, tsrc, t_e in ((0, d_t2, t_el), (1, d_t2h, t_eh)):
            patf = tsrc[:, :].bitcast(BF16)
            dst = d_wn[:, koff * 1024:(koff + 1) * 1024]
            t_sm = pr.emit("vector", lambda v, dst=dst, patf=patf, scee=scee:
                           v.tensor_tensor(dst, patf, scee, ALU.mult),
                           waits=[t_e, t_sce, wst["tok_wn"][koff]], inc="v",
                           tag=f"Wsm{ui}_{koff}")
            sm_toks.append(t_sm)
        wst["tok_sm"][buf] = sm_toks[1]
        wst["last_sm"] = sm_toks[1]
        if mat == "d":
            rr = scr.rearrange("(c i p two) d -> p c i two d", c=8, i=8, p=P, two=2)
        else:
            rr = scr.rearrange("(i p two) n -> p i two n", i=8, p=P, two=2)
        for koff in (0, 1):
            stg = d_wt if koff == 0 else d_wt_hi
            ev_toks = []
            last_mm = None
            for bg in range(2):
                b = 2 * koff + bg
                for j4 in range(4):
                    jj = koff * 8 + bg * 4 + j4
                    t_mm = pr.emit("tensor", lambda t, jj=jj, b=b, j4=j4: t.matmul(
                        psA[b][:, j4 * P:(j4 + 1) * P], d_wn[:, jj * P:(jj + 1) * P],
                        ident[:], start=True, stop=True),
                        waits=[sm_toks[koff], T_ID] + ([bk.gate(b)] if j4 == 0 else []),
                        inc="pe", tag=f"WT{ui}_{jj}")
                    last_mm = t_mm
                t_ev = pr.emit("scalar", lambda s, bg=bg, b=b, stg=stg: s.copy(
                    stg[:, bg * 512:(bg + 1) * 512], psA[b][:, 0:512]),
                    waits=[t_mm, wst["tok_store"][koff] if bg == 0 else None], inc="s",
                    tag=f"WTe{ui}_{koff}_{bg}")
                bk.done(b, t_ev)
                ev_toks.append(t_ev)
            wst["tok_wn"][koff] = last_mm
            if mat == "d":
                dst3 = rr[:, cc, :, koff, rt * P:(rt + 1) * P]
            else:
                dst3 = rr[:, :, koff, rt * P:(rt + 1) * P]
            t_st = pr.dma("scalar", dst3, stg[:, :].rearrange("p (i c) -> p i c", i=8),
                          waits=[F(t) for t in ev_toks], tag=f"Wst{ui}_{koff}",
                          stream=f"wst{koff}")
            wst["tok_store"][koff] = t_st
            wst["last_store"] = t_st
        wst["last_mm"] = wst["tok_wn"][1]
        return True

    def pump(n):
        for _ in range(n):
            if not emit_w_unit():
                return

    def drain_w():
        while emit_w_unit():
            pass

    if "A" not in phases:
        drain_w()
    T_W_V = None
    T_W_REL = []

    for nm in ("scrg", "scrd"):
        ap = dbg_out(nm, [D, 512] if nm == "scrg" else [512, D], BF16)
        if ap is not None:
            drain_w()
            pr.dma("sync", ap, scr_g[:, 0:512] if nm == "scrg" else scr_d[0:512, :],
                   waits=[wst["last_store"]], tag="dbg" + nm)
    ap = dbg_out("scrd0", [NF, P], BF16)
    if ap is not None:
        drain_w()
        pr.dma("sync", ap, scr_d[:, 0:P], waits=[wst["last_store"]], tag="dbgscrd0")

    def emit_ln(x_ap, out_ap, waits, tag):
        t_sum = pr.emit("vector", lambda v: v.reduce_sum(stats[:, 0:1], x_ap, axis=AX.X),
                        waits=waits, inc="v", tag=tag + "sum")
        t_sq = pr.emit("scalar", lambda s: s.activation(xnnat[:], x_ap, AF.Square,
                                                        accum_out=stats[:, 1:2]),
                       waits=waits, inc="s", tag=tag + "sq")
        t_mean = pr.emit("vector", lambda v: v.tensor_scalar(
            stats[:, 2:3], stats[:, 0:1], 1.0 / D, None, ALU.mult), waits=[F(t_sum)], inc="v")
        t_e2 = pr.emit("vector", lambda v: v.tensor_scalar(
            stats[:, 3:4], stats[:, 1:2], 1.0 / D, None, ALU.mult), waits=[t_sq], inc="v")
        t_var = pr.emit("vector", lambda v: v.scalar_tensor_tensor(
            stats[:, 4:5], stats[:, 2:3], stats[:, 2:3], stats[:, 3:4], ALU.mult, ALU.subtract),
            waits=[F(t_mean), F(t_e2)], inc="v", tag=tag + "var")
        t_sd = pr.emit("scalar", lambda s: s.activation(stats[:, 5:6], stats[:, 4:5], AF.Sqrt,
                                                        bias=eps_t[:], scale=-1.0),
                       waits=[t_var], inc="s")
        t_rstd = pr.emit("vector", lambda v: v.reciprocal(stats[:, 6:7], stats[:, 5:6]),
                         waits=[t_sd], inc="v")
        t_b = pr.emit("vector", lambda v: v.tensor_scalar(
            stats[:, 7:8], stats[:, 2:3], stats[:, 6:7], -1.0, ALU.mult, ALU.mult),
            waits=[F(t_rstd)], inc="v")
        t_xn = pr.emit("vector", lambda v: v.tensor_scalar(
            out_ap, x_ap, stats[:, 6:7], stats[:, 7:8], ALU.mult, ALU.add),
            waits=[F(t_b), t_sq], inc="v", tag=tag + "xn")
        return t_xn, t_sd

    tok_stats = None
    T_X2 = None
    T_AOT = None
    if "A" in phases:
        drain_w()
        tok_xtile = [None, None]
        tok_xnnat = None
        for tt in range(TTILES):
            buf = tt % 2
            xt = xtile[:, buf * D:(buf + 1) * D]
            src = d_in["x_halo"] if tt == 0 else d_in["x_loc"][(tt - 1) * P:tt * P, :]
            t_ld = pr.dma("sync", xt, src, waits=[tok_xtile[buf], T_W_V], tag=f"A1ld{tt}", stream=f"a1x{buf}")
            t_xn, t_sd = emit_ln(xt, xnnat[:], [t_ld, tok_xnnat, tok_stats, T_W_V], f"A1_{tt}")
            tok_stats = t_sd
            tok_xtile[buf] = t_xn
            for q4 in range(4):
                b = 4 + q4 % 2
                for k4 in range(4):
                    i = q4 * 4 + k4
                    t_mm = pr.emit("tensor", lambda t, i=i, b=b, k4=k4: t.matmul(
                        psA[b][:, k4 * P:(k4 + 1) * P], xnnat[:, i * P:(i + 1) * P], ident[:],
                        start=True, stop=True),
                        waits=[t_xn, T_ID] + ([bk.gate(b)] if k4 == 0 else []),
                        inc="pe", tag=f"A1T{tt}_{q4}_{k4}")
                def eva1(v, q4=q4, b=b, tt=tt):
                    d3 = xnT[:, q4 * 4 * TT:(q4 * 4 + 4) * TT] \
                        .rearrange("p (i r) -> p i r", i=4)[:, :, tt * P:(tt + 1) * P]
                    return v.tensor_copy(d3, psA[b][:, 0:512].rearrange("p (i r) -> p i r", i=4))
                t_ev = pr.emit("vector", eva1, waits=[t_mm], inc="v", tag=f"A1E{tt}_{q4}")
                bk.done(b, t_ev)
            tok_xnnat = t_mm
            pump(1)
        TOK_XNT = pr.marker("vector")

        ap = dbg_out("xnT", [P, DT * TT], BF16)
        if ap is not None:
            pr.dma("sync", ap, xnT, waits=[TOK_XNT], tag="dbgxnT")

        tok_wrow = [None, None]
        tok_wrb = [None, None]
        tok_qkv = {"q": None, "k": None, "v": None}
        tok_expb = [None, None]
        tok_rbc = None
        t_ao = None

        for g in range(G):
            for which, wsrc, dstbuf in (("q", d_in["wq"], qT), ("k", d_in["wk"], kT)):
                for pt in range(2):
                    buf = pt % 2
                    wr = wrow[:, buf * D:(buf + 1) * D]
                    wrb = wrow_b[:, buf * D:(buf + 1) * D]
                    r0 = DPG * g + P * pt
                    t_ld = pr.dma("sync", wr, wsrc[r0:r0 + P, :], waits=[tok_wrow[buf]],
                                  tag=f"A2ld{which}{g}{pt}", stream=f"wr{buf}")
                    t_cast = pr.emit("scalar", lambda s, wr=wr, wrb=wrb: s.copy(wrb, wr),
                                     waits=[t_ld, tok_wrb[buf]], inc="s")
                    tok_wrow[buf] = t_cast
                    ev_toks = []
                    for q4 in range(4):
                        b = q4 % 2
                        for k4 in range(4):
                            i = q4 * 4 + k4
                            t_mm = pr.emit("tensor", lambda t, wrb=wrb, i=i, b=b, k4=k4: t.matmul(
                                psA[b][:, k4 * P:(k4 + 1) * P], wrb[:, i * P:(i + 1) * P], ident[:],
                                start=True, stop=True),
                                waits=[t_cast, T_ID] + ([bk.gate(b)] if k4 == 0 else []),
                                inc="pe", tag=f"A2T{which}{g}{pt}_{i}")
                        t_ev = pr.emit("vector", lambda v, q4=q4, b=b: v.tensor_copy(
                            wT_a[:, q4 * 512:(q4 + 1) * 512], psA[b][:, 0:512]),
                            waits=[t_mm], inc="v")
                        bk.done(b, t_ev)
                        ev_toks.append(t_ev)
                    tok_wrb[buf] = t_mm
                    if g == G - 1 and which == "q" and pt == 0:
                        ap = dbg_out("wTa", [P, DT * P], BF16)
                        if ap is not None:
                            pr.dma("sync", ap, wT_a, waits=[ev_toks[-1]], tag="dbgwTa")
                    last_c = {}
                    for i in range(DT):
                        for ci, (c0, cw) in enumerate(TCH):
                            b = 2 + ci
                            t_mm = pr.emit("tensor", lambda t, i=i, b=b, c0=c0, cw=cw: t.matmul(
                                psA[b][:, 0:cw], wT_a[:, i * P:(i + 1) * P],
                                xnT[:, i * TT + c0: i * TT + c0 + cw],
                                start=(i == 0), stop=(i == DT - 1)),
                                waits=([ev_toks[-1], TOK_XNT, bk.gate(b)] if i == 0 else []),
                                inc="pe", tag=f"A2M{which}{g}{pt}_{c0}_{i}")
                            last_c[ci] = t_mm
                    for ci, (c0, cw) in enumerate(TCH):
                        b = 2 + ci
                        t_ev = pr.emit("vector", lambda v, b=b, pt=pt, c0=c0, cw=cw, dstbuf=dstbuf:
                                       v.tensor_copy(dstbuf[:, pt * TT + c0: pt * TT + c0 + cw],
                                                     psA[b][:, 0:cw]),
                                       waits=[last_c[ci]], inc="v", tag=f"A2E{which}{g}{pt}_{c0}")
                        bk.done(b, t_ev)
                    pump(3)
                tok_qkv[which] = t_ev
            ev_toks_v = []
            for pt in range(2):
                buf = pt % 2
                wr = wrow[:, buf * D:(buf + 1) * D]
                wrb = wrow_b[:, buf * D:(buf + 1) * D]
                r0 = DPG * g + P * pt
                t_ld = pr.dma("sync", wr, d_in["wv"][r0:r0 + P, :], waits=[tok_wrow[buf]],
                              tag=f"A2ldv{g}{pt}", stream=f"wr{buf}")
                t_cast = pr.emit("scalar", lambda s, wr=wr, wrb=wrb: s.copy(wrb, wr),
                                 waits=[t_ld, tok_wrb[buf]], inc="s")
                tok_wrow[buf] = t_cast
                for q4 in range(4):
                    b = q4 % 2
                    for k4 in range(4):
                        i = q4 * 4 + k4
                        t_mm = pr.emit("tensor", lambda t, wrb=wrb, i=i, b=b, k4=k4: t.matmul(
                            psA[b][:, k4 * P:(k4 + 1) * P], wrb[:, i * P:(i + 1) * P], ident[:],
                            start=True, stop=True),
                            waits=[t_cast, T_ID] + ([bk.gate(b)] if k4 == 0 else []),
                            inc="pe", tag=f"A2Tv{g}{pt}_{i}")
                    def evv(v, q4=q4, pt=pt, b=b):
                        d3 = wT_v[:, q4 * 4 * DPG:(q4 * 4 + 4) * DPG] \
                            .rearrange("p (i r) -> p i r", i=4)[:, :, pt * P:(pt + 1) * P]
                        return v.tensor_copy(d3, psA[b][:, 0:512].rearrange("p (i r) -> p i r", i=4))
                    t_ev = pr.emit("vector", evv, waits=[t_mm], inc="v")
                    bk.done(b, t_ev)
                    ev_toks_v.append(t_ev)
                tok_wrb[buf] = t_mm
            for tt in range(TTILES):
                b = 2 + tt % 4
                for i in range(DT):
                    t_mm = pr.emit("tensor", lambda t, i=i, b=b, tt=tt: t.matmul(
                        psA[b][:, 0:DPG], xnT[:, i * TT + tt * P: i * TT + (tt + 1) * P],
                        wT_v[:, i * DPG:(i + 1) * DPG],
                        start=(i == 0), stop=(i == DT - 1)),
                        waits=([ev_toks_v[-1], TOK_XNT, bk.gate(b)] if i == 0 else []),
                        inc="pe", tag=f"A2Mv{g}{tt}_{i}")
                t_ev = pr.emit("vector", lambda v, b=b, tt=tt: v.tensor_copy(
                    vN[:, tt * DPG:(tt + 1) * DPG], psA[b][:, 0:DPG]),
                    waits=[t_mm], inc="v", tag=f"A2Ev{g}{tt}")
                bk.done(b, t_ev)
                pump(1)
            tok_qkv["v"] = t_ev
            T_QKV = [tok_qkv["q"], tok_qkv["k"], tok_qkv["v"]]

            if g == G - 1:
                for nm, bufp, sz in (("qT", qT, 2 * TT), ("kT", kT, 2 * TT),
                                     ("vN", vN, TTILES * DPG)):
                    ap = dbg_out(nm, [P, sz], BF16)
                    if ap is not None:
                        pr.dma("sync", ap, bufp, waits=T_QKV, tag="dbg" + nm)

            if g == 5 and "W" in phases:
                drain_w()
                T_W_REL = [wst["last_store"], wst["last_mm"], wst["last_sm"]]
            for ib in range(NB):
                msel = 0 if ib == 0 else 1
                t_den_mms = []
                t_av = {}
                for hp in range(HG):
                    base = DK * (hp % 2)
                    ptile = hp // 2
                    sbank = 4 + hp % 2
                    slot = hp % 2
                    exp_ap = expb[:, slot * 2 * P:(slot + 1) * 2 * P]
                    for cj in range(2):
                        def mm_sc(t, cj=cj, base=base, ptile=ptile, sbank=sbank, ib=ib):
                            return t.matmul(
                                psA[sbank][:, cj * P:(cj + 1) * P],
                                kT[base:base + DK,
                                   ptile * TT + (ib + cj) * P: ptile * TT + (ib + cj + 1) * P],
                                qT[base:base + DK,
                                   ptile * TT + (ib + 1) * P: ptile * TT + (ib + 2) * P],
                                start=True, stop=True, tile_position=(base, 0))
                        mm2 = pr.emit("tensor", mm_sc,
                                      waits=[bk.gate(sbank)] + (T_QKV if (ib == 0 and hp < 2) else []),
                                      inc="pe", tag=f"A3S{g}_{ib}_{hp}_{cj}")
                    t_exp = pr.emit("scalar", lambda s, exp_ap=exp_ap, sbank=sbank: s.activation(
                        exp_ap, psA[sbank][:, 0:2 * P], AF.Exp, bias=c20[:], scale=0.125),
                        waits=[mm2, T_C20, tok_expb[slot]], inc="s", tag=f"A3X{g}_{ib}_{hp}")
                    bk.done(sbank, t_exp)
                    t_msk = pr.emit("vector", lambda v, exp_ap=exp_ap, msel=msel: v.tensor_tensor(
                        exp_ap, exp_ap, maskTs[:, msel * 2 * P:(msel + 1) * 2 * P], ALU.mult),
                        waits=[t_exp, T_MSK0, T_MSK1], inc="v", tag=f"A3M{g}_{ib}_{hp}")
                    for cj in range(2):
                        t_den = pr.emit("tensor", lambda t, cj=cj, hp=hp, exp_ap=exp_ap: t.matmul(
                            psA[6][0:1, hp * P:(hp + 1) * P], ones_col[:],
                            exp_ap[:, cj * P:(cj + 1) * P],
                            start=(cj == 0), stop=(cj == 1)),
                            waits=[t_msk] + ([bk.gate(6)] if (cj == 0 and hp == 0) else []),
                            inc="pe", tag=f"A3D{g}_{ib}_{hp}_{cj}")
                    t_den_mms.append(t_den)
                    vbank = 7 if hp < 2 else 3
                    for cj in range(2):
                        def mm_av(t, cj=cj, hp=hp, base=base, ib=ib, vbank=vbank, exp_ap=exp_ap):
                            return t.matmul(
                                psA[vbank][base:base + DK, 0:P],
                                vN[:, (ib + cj) * DPG + hp * DK: (ib + cj) * DPG + (hp + 1) * DK],
                                exp_ap[:, cj * P:(cj + 1) * P],
                                start=(cj == 0), stop=(cj == 1),
                                tile_position=(0, base))
                        t_avm = pr.emit("tensor", mm_av,
                                        waits=[t_msk] + ([bk.gate(vbank)] if (cj == 0 and hp % 2 == 0) else []),
                                        inc="pe", tag=f"A3V{g}_{ib}_{hp}_{cj}")
                    t_av[hp] = t_avm
                    tok_expb[slot] = t_avm
                t_rec = pr.emit("vector", lambda v: v.reciprocal(recip[:], psA[6][0:1, 0:HG * P]),
                                waits=t_den_mms, inc="v", tag=f"A3R{g}_{ib}")
                bk.done(6, t_rec)
                for p2 in range(2):
                    vbank = 7 if p2 == 0 else 3
                    for par in range(2):
                        hp = 2 * p2 + par
                        def mm_bc(t, par=par, hp=hp, vbank=vbank):
                            return t.matmul(
                                psA[vbank][par * DK:(par + 1) * DK, P:2 * P],
                                ones_r64[:], recip[0:1, hp * P:(hp + 1) * P],
                                start=True, stop=True, tile_position=(0, par * DK))
                        t_bc = pr.emit("tensor", mm_bc, waits=[t_rec], inc="pe",
                                       tag=f"A3B{g}_{ib}_{hp}")
                    rb = rbc[:, p2 * P:(p2 + 1) * P]
                    t_rbc = pr.emit("vector", lambda v, vbank=vbank, rb=rb: v.tensor_copy(
                        rb, psA[vbank][:, P:2 * P]),
                        waits=[t_bc, tok_rbc], inc="v")
                    dto = 2 * g + p2
                    t_ao = pr.emit("vector", lambda v, rb=rb, vbank=vbank, dto=dto, ib=ib:
                                   v.tensor_tensor(
                                       aoT[:, dto * TLOC + ib * P: dto * TLOC + (ib + 1) * P],
                                       psA[vbank][:, 0:P], rb, ALU.mult),
                                   waits=[t_av[2 * p2], t_av[2 * p2 + 1], F(t_rbc)] +
                                         (T_W_REL if dto >= 11 else []), inc="v",
                                   tag=f"A3O{g}_{ib}_{p2}")
                    bk.done(vbank, t_ao)
                    tok_rbc = t_ao
                pump(3)

        T_AOT = pr.marker("vector")
        ap = dbg_out("aoT", [P, DT * TLOC], BF16)
        if ap is not None:
            pr.dma("sync", ap, aoT, waits=[T_AOT], tag="dbgaoT")

        tok_projb = None
        tok_xrld = None
        last_add = None
        for dj in range(DT):
            buf = dj % 2
            wr = wrow[:, buf * D:(buf + 1) * D]
            wrb = wrow_b[:, buf * D:(buf + 1) * D]
            t_ld = pr.dma("sync", wr, d_in["wo"][dj * P:(dj + 1) * P, :],
                          waits=[tok_wrow[buf]], tag=f"A4ld{dj}", stream=f"wr{buf}")
            t_cast = pr.emit("scalar", lambda s, wr=wr, wrb=wrb: s.copy(wrb, wr),
                             waits=[t_ld, tok_wrb[buf]], inc="s")
            tok_wrow[buf] = t_cast
            ev_toks = []
            for m in range(DT):
                b = m % 2
                t_mm = pr.emit("tensor", lambda t, wrb=wrb, m=m, b=b: t.matmul(
                    psA[b][:, 0:P], wrb[:, m * P:(m + 1) * P], ident[:],
                    start=True, stop=True),
                    waits=[t_cast, T_ID, bk.gate(b)], inc="pe", tag=f"A4T{dj}_{m}")
                t_ev = pr.emit("vector", lambda v, m=m, b=b: v.tensor_copy(
                    wT_a[:, m * P:(m + 1) * P], psA[b][:, 0:P]),
                    waits=[t_mm], inc="v")
                bk.done(b, t_ev)
                ev_toks.append(t_ev)
            tok_wrb[buf] = t_mm
            for tc in range(4):
                b = 2
                for m in range(DT):
                    t_mm = pr.emit("tensor", lambda t, m=m, tc=tc, b=b: t.matmul(
                        psA[b][:, 0:512], wT_a[:, m * P:(m + 1) * P],
                        aoT[:, m * TLOC + tc * 512: m * TLOC + (tc + 1) * 512],
                        start=(m == 0), stop=(m == DT - 1)),
                        waits=([ev_toks[-1], T_AOT, bk.gate(b)] if m == 0 else []),
                        inc="pe", tag=f"A4M{dj}_{tc}_{m}")
                t_pb = pr.emit("scalar", lambda s, b=b: s.copy(projb[:], psA[b][:, 0:512]),
                               waits=[t_mm, tok_projb], inc="s", tag=f"A4P{dj}_{tc}")
                bk.done(b, t_pb)
                t_xr = pr.dma("sync", xrld[:],
                              d_in["x_loc"][tc * 512:(tc + 1) * 512, dj * P:(dj + 1) * P]
                              .rearrange("(a p) b -> p a b", p=P),
                              waits=[tok_xrld], tag=f"A4xr{dj}_{tc}", stream="xr")
                for s4 in range(4):
                    b2 = 4 + s4
                    t_mm2 = pr.emit("tensor", lambda t, s4=s4, b2=b2: t.matmul(
                        psA[b2][:, 0:P], projb[:, s4 * P:(s4 + 1) * P], ident[:],
                        start=True, stop=True),
                        waits=[t_pb, bk.gate(b2)], inc="pe", tag=f"A4N{dj}_{tc}_{s4}")
                    tti = tc * 4 + s4
                    t_add = pr.emit("vector", lambda v, s4=s4, b2=b2, tti=tti, dj=dj:
                                    v.tensor_tensor(
                                        x2[:, tti * D + dj * P: tti * D + (dj + 1) * P],
                                        psA[b2][:, 0:P], xrld[:, s4 * P:(s4 + 1) * P], ALU.add),
                                    waits=[t_mm2, t_xr], inc="v",
                                    tag=f"A4A{dj}_{tc}_{s4}")
                    bk.done(b2, t_add)
                    last_add = t_add
                tok_projb = t_mm2
                tok_xrld = last_add
        T_X2 = pr.marker("vector")
        ap = dbg_out("x2", [P, NB * D], BF16)
        if ap is not None:
            pr.dma("sync", ap, x2, waits=[T_X2], tag="dbgx2")

    if "W" in phases:
        drain_w()
        T_W_DONE = [wst["last_store"]]
        for nm, scr_, shp in (("scrgZ", scr_g, [D, NF]), ("scruZ", scr_u, [D, NF]),
                              ("scrdZ", scr_d, [NF, D])):
            ap = dbg_out(nm, shp, BF16)
            if ap is not None:
                pr.dma("sync", ap, scr_[:, :], waits=T_W_DONE, tag="dbg" + nm)
    else:
        T_W_DONE = []
    if "B" in phases:
        st_toks = []
        tok_wgt = [None, None]
        tok_wut = [None, None]
        tok_wdt = [None, None]
        tok_g1 = None
        tok_xn2 = None
        tok_otile = None
        for slab in range(2):
            tok_acc = {}
            for u in range(8):
                tti = slab * 8 + u
                x2t = x2[:, tti * D:(tti + 1) * D]
                t_xn, t_sd = emit_ln(x2t, xnnat[:], [T_X2, tok_xn2, tok_stats], f"B1_{slab}_{u}")
                tok_stats = t_sd
                for q4 in range(4):
                    b = q4 % 2
                    for k4 in range(4):
                        i = q4 * 4 + k4
                        t_mm = pr.emit("tensor", lambda t, i=i, b=b, k4=k4: t.matmul(
                            psA[b][:, k4 * P:(k4 + 1) * P], xnnat[:, i * P:(i + 1) * P], ident[:],
                            start=True, stop=True),
                            waits=[t_xn, T_ID] + ([bk.gate(b)] if k4 == 0 else []),
                            inc="pe", tag=f"B1T{slab}_{u}_{q4}_{k4}")
                    def evx(v, q4=q4, b=b, u=u):
                        d3 = xn2T[:, q4 * 4 * SLAB:(q4 * 4 + 4) * SLAB] \
                            .rearrange("p (i r) -> p i r", i=4)[:, :, u * P:(u + 1) * P]
                        return v.tensor_copy(d3, psA[b][:, 0:512].rearrange("p (i r) -> p i r", i=4))
                    t_ev = pr.emit("vector", evx, waits=[t_mm], inc="v")
                    bk.done(b, t_ev)
                tok_xn2 = t_mm
                for q4 in range(4):
                    b = 4 + q4 % 2
                    for k4 in range(4):
                        i = q4 * 4 + k4
                        t_mm = pr.emit("tensor", lambda t, i=i, b=b, k4=k4, x2t=x2t: t.matmul(
                            psA[b][:, k4 * P:(k4 + 1) * P], x2t[:, i * P:(i + 1) * P], ident[:],
                            start=True, stop=True),
                            waits=[T_X2, T_ID] + ([bk.gate(b)] if k4 == 0 else []),
                            inc="pe", tag=f"B1A{slab}_{u}_{q4}_{k4}")
                    def eva(v, q4=q4, b=b, u=u):
                        d3 = aoT[:, q4 * 4 * SLAB:(q4 * 4 + 4) * SLAB] \
                            .rearrange("p (i r) -> p i r", i=4)[:, :, u * P:(u + 1) * P]
                        return v.tensor_copy(d3, psA[b][:, 0:512].rearrange("p (i r) -> p i r", i=4))
                    t_ev = pr.emit("vector", eva, waits=[t_mm], inc="v",
                                   tag=f"B1AE{slab}_{u}_{q4}")
                    bk.done(b, t_ev)
            TOK_XN2 = pr.marker("vector")
            for nm in (f"acc{slab}i", f"xn2{slab}"):
                ap = dbg_out(nm, [P, ACCOFF], BF16)
                if ap is not None:
                    pr.dma("sync", ap, aoT[:, 0:ACCOFF] if nm.startswith("acc") else xn2T,
                           waits=[TOK_XN2], tag="dbg" + nm)

            for sup in range(8):
                inter_toks = []
                for ch in range(16):
                    c0 = sup * 2048 + ch * P
                    buf = ch % 2
                    wg = wgt[:, buf * 2048:(buf + 1) * 2048]
                    wu = wut[:, buf * 2048:(buf + 1) * 2048]
                    t_wg = pr.dma("sync", wg,
                                  scr_g[:, c0:c0 + P].rearrange("(i p) n -> p i n", p=P),
                                  waits=[tok_wgt[buf], T_X2] + T_W_DONE,
                                  tag=f"B2g{slab}_{sup}_{ch}", stream=f"bg{buf}")
                    t_wu = pr.dma("scalar", wu,
                                  scr_u[:, c0:c0 + P].rearrange("(i p) n -> p i n", p=P),
                                  waits=[tok_wut[buf], T_X2] + T_W_DONE,
                                  tag=f"B2u{slab}_{sup}_{ch}", stream=f"bu{buf}")
                    ntl = ch
                    lastg = {}
                    lastu = {}
                    for i in range(DT):
                        for tc in range(2):
                            t_mmg = pr.emit("tensor", lambda t, i=i, tc=tc, wg=wg: t.matmul(
                                psA[tc][:, 0:512], wg[:, i * P:(i + 1) * P],
                                xn2T[:, i * SLAB + tc * 512: i * SLAB + (tc + 1) * 512],
                                start=(i == 0), stop=(i == DT - 1)),
                                waits=([t_wg, TOK_XN2, bk.gate(tc)] if i == 0 else []),
                                inc="pe", tag=f"B2G{slab}_{sup}_{ntl}_{tc}_{i}")
                            lastg[tc] = t_mmg
                    for i in range(DT):
                        for tc in range(2):
                            t_mmu = pr.emit("tensor", lambda t, i=i, tc=tc, wu=wu: t.matmul(
                                psA[2 + tc][:, 0:512], wu[:, i * P:(i + 1) * P],
                                xn2T[:, i * SLAB + tc * 512: i * SLAB + (tc + 1) * 512],
                                start=(i == 0), stop=(i == DT - 1)),
                                waits=([t_wu, TOK_XN2, bk.gate(2 + tc)] if i == 0 else []),
                                inc="pe", tag=f"B2U{slab}_{sup}_{ntl}_{tc}_{i}")
                            lastu[tc] = t_mmu
                    for tc in range(2):
                        t_gelu = pr.emit("scalar", lambda s, tc=tc: s.activation(
                            g1[:], psA[tc][:, 0:512], AF.Gelu_apprx_tanh),
                            waits=[lastg[tc], tok_g1], inc="s", tag=f"B2A{slab}_{sup}_{ntl}_{tc}")
                        bk.done(tc, t_gelu)
                        t_int = pr.emit("vector", lambda v, ntl=ntl, tc=tc: v.tensor_tensor(
                            aoT[:, ACCOFF + ntl * SLAB + tc * 512: ACCOFF + ntl * SLAB + (tc + 1) * 512],
                            psA[2 + tc][:, 0:512], g1[:], ALU.mult),
                            waits=[lastu[tc], t_gelu], inc="v", tag=f"B2I{slab}_{sup}_{ntl}_{tc}")
                        bk.done(2 + tc, t_int)
                        tok_g1 = t_int
                        inter_toks.append(t_int)
                    tok_wgt[buf] = lastg[1]
                    tok_wut[buf] = lastu[1]
                for ch in range(16):
                    buf = ch % 2
                    wd = wdt[:, buf * 2048:(buf + 1) * 2048]
                    t_wd = pr.dma("scalar", wd,
                                  scr_d[sup * 2048:(sup + 1) * 2048, ch * P:(ch + 1) * P]
                                  .rearrange("(i p) n -> p i n", p=P),
                                  waits=[tok_wdt[buf], T_X2] + T_W_DONE,
                                  tag=f"B3d{slab}_{sup}_{ch}", stream=f"bd{buf}")
                    hti = ch
                    lastd = {}
                    for i in range(DT):
                        for tc in range(2):
                            b = 4 + 2 * (ch % 2) + tc
                            t_mmd = pr.emit("tensor", lambda t, i=i, tc=tc, b=b, wd=wd: t.matmul(
                                psA[b][:, 0:512], wd[:, i * P:(i + 1) * P],
                                aoT[:, ACCOFF + i * SLAB + tc * 512: ACCOFF + i * SLAB + (tc + 1) * 512],
                                start=(i == 0), stop=(i == DT - 1)),
                                waits=([t_wd, bk.gate(b)] + inter_toks if i == 0 else []),
                                inc="pe", tag=f"B3M{slab}_{sup}_{hti}_{tc}_{i}")
                            lastd[tc] = t_mmd
                    if slab == 1 and sup == 0 and ch == 0:
                        ap = dbg_out("dpsum", [P, 512], F32)
                        if ap is not None:
                            t_cp = pr.emit("vector", lambda v: v.tensor_copy(
                                otile[:, 0:512], psA[4][:, 0:512]),
                                waits=[lastd[0], lastd[1]], inc="v", tag="dbgdpsumcp")
                            pr.dma("sync", ap, otile[:, 0:512], waits=[t_cp], tag="dbgdpsum")
                        ap = dbg_out("dwdt", [P, DT * P], BF16)
                        if ap is not None:
                            t_cpw = pr.emit("vector", lambda v, wd=wd: v.tensor_copy(
                                xnnat[:], wd), waits=[t_wd], inc="v", tag="dbgdwdtcp")
                            pr.dma("sync", ap, xnnat[:], waits=[t_cpw], tag="dbgdwdt")
                    for tc in range(2):
                        b = 4 + 2 * (ch % 2) + tc
                        key = (hti, tc)
                        t_acc = pr.emit("vector", lambda v, hti=hti, tc=tc, b=b: v.tensor_tensor(
                            aoT[:, hti * SLAB + tc * 512: hti * SLAB + (tc + 1) * 512],
                            psA[b][:, 0:512],
                            aoT[:, hti * SLAB + tc * 512: hti * SLAB + (tc + 1) * 512],
                            ALU.add),
                            waits=[lastd[tc], tok_acc.get(key)], inc="v",
                            tag=f"B3A{slab}_{sup}_{hti}_{tc}")
                        bk.done(b, t_acc)
                        tok_acc[key] = t_acc
                    tok_wdt[buf] = lastd[1]
                for nm, rg0, rg1 in ((f"accS{slab}_{sup}", 0, ACCOFF),
                                     (f"intS{slab}_{sup}", ACCOFF, 2 * ACCOFF)):
                    ap = dbg_out(nm, [P, ACCOFF], BF16)
                    if ap is not None:
                        toks = [tok_acc[kk2] for kk2 in tok_acc]
                        pr.dma("sync", ap, aoT[:, rg0:rg1], waits=toks, tag="dbg" + nm)
            for u in range(8):
                for half in range(2):
                    ev_last = None
                    for q4 in range(2):
                        b = q4 % 2
                        for k4 in range(4):
                            i = half * 8 + q4 * 4 + k4
                            t_mm = pr.emit("tensor", lambda t, i=i, u=u, b=b, k4=k4: t.matmul(
                                psA[b][:, k4 * P:(k4 + 1) * P],
                                aoT[:, i * SLAB + u * P: i * SLAB + (u + 1) * P],
                                ident[:], start=True, stop=True),
                                waits=[tok_acc.get((i, u // 4)), T_ID] +
                                      ([bk.gate(b), tok_otile] if k4 == 0 else []),
                                inc="pe", tag=f"B4T{slab}_{u}_{i}")
                        t_ev = pr.emit("vector", lambda v, q4=q4, b=b: v.tensor_copy(
                            otile[:, q4 * 512:(q4 + 1) * 512], psA[b][:, 0:512]),
                            waits=[t_mm], inc="v")
                        bk.done(b, t_ev)
                        ev_last = t_ev
                    t_st = pr.dma("sync",
                                  out_t[slab * SLAB + u * P: slab * SLAB + (u + 1) * P,
                                        half * 1024:(half + 1) * 1024],
                                  otile[:], waits=[ev_last], tag=f"B4S{slab}_{u}_{half}",
                                  stream="bst")
                    st_toks.append(t_st)
                    tok_otile = t_st
        pr.emit("gpsimd", lambda gq: gq.engine_nop(), waits=[st_toks[-1]], inc="g", tag="final")
    else:
        allw = [(s, pr.cnt[s], "dma") for s in pr.cnt if s.startswith("dma_") and pr.cnt[s]]
        pr.emit("gpsimd", lambda gq: gq.engine_nop(), waits=allw, inc="g", tag="final")

    return nc, ctx, pr, d_in, out_t, dbg


def finish(nc, ctx, pr):
    block = ctx.enter_context(nc.Block())
    pr.build(block)
    ctx.close()
    return nc


from concourse.bass_utils import run_bass_kernel_spmd
import ml_dtypes

NCORES = 8

_mask_cache = {}


def _masks():
    if "m" not in _mask_cache:
        r = np.arange(P)[None, :]
        c = np.arange(2 * P)[:, None]
        rest = ((c >= r) & (c <= r + P)).astype(np.float32)
        first = rest * (c >= P)
        _mask_cache["m"] = (first.astype(ml_dtypes.bfloat16),
                           rest.astype(ml_dtypes.bfloat16))
    return _mask_cache["m"]


_prog_cache = {}


def _get_nc():
    if "nc" not in _prog_cache:
        nc, ctx, pr, d_in, out_t, dbg = build_program(phases=("W", "A", "B"), debug=())
        block = ctx.enter_context(nc.Block())
        pr.build(block)
        ctx.close()
        _prog_cache["nc"] = nc
    return _prog_cache["nc"]


def kernel(L0_attn_x, L0_attn_Wq, L0_attn_Wk, L0_attn_Wv, L0_attn_Wo,
           L0_ffn_gate_W_packed, L0_ffn_gate_scales,
           L0_ffn_up_W_packed, L0_ffn_up_scales,
           L0_ffn_down_W_packed, L0_ffn_down_scales):
    x = np.ascontiguousarray(np.asarray(L0_attn_x, dtype=np.float32).reshape(NCORES * TLOC, D))
    wq = np.ascontiguousarray(np.asarray(L0_attn_Wq, dtype=np.float32))
    wk = np.ascontiguousarray(np.asarray(L0_attn_Wk, dtype=np.float32))
    wv = np.ascontiguousarray(np.asarray(L0_attn_Wv, dtype=np.float32))
    wo = np.ascontiguousarray(np.asarray(L0_attn_Wo, dtype=np.float32))
    def _bytes(a):
        a = np.ascontiguousarray(np.asarray(a, dtype=np.int32))
        return np.ascontiguousarray(a.view(np.uint8).reshape(a.shape[0], -1)[:, 0::4])
    gp = _bytes(L0_ffn_gate_W_packed)
    gs = np.ascontiguousarray(np.asarray(L0_ffn_gate_scales, dtype=np.float32))
    up = _bytes(L0_ffn_up_W_packed)
    us = np.ascontiguousarray(np.asarray(L0_ffn_up_scales, dtype=np.float32))
    dp = _bytes(L0_ffn_down_W_packed)
    ds = np.ascontiguousarray(np.asarray(L0_ffn_down_scales, dtype=np.float32))
    mfirst, mrest = _masks()
    zero_halo = np.zeros((HALO, D), dtype=np.float32)

    in_maps = []
    for c in range(NCORES):
        halo = zero_halo if c == 0 else x[c * TLOC - HALO:c * TLOC]
        in_maps.append(dict(
            x_loc=x[c * TLOC:(c + 1) * TLOC],
            x_halo=np.ascontiguousarray(halo),
            wq=wq, wk=wk, wv=wv, wo=wo,
            gate_p=gp, gate_s=gs, up_p=up, up_s=us, down_p=dp, down_s=ds,
            maskT_first=(mfirst if c == 0 else mrest),
            maskT_rest=mrest,
        ))

    nc = _get_nc()
    res = run_bass_kernel_spmd(nc, in_maps, core_ids=list(range(NCORES)))
    out = np.concatenate([np.asarray(r["out"], dtype=np.float32) for r in res.results], axis=0)
    return out.reshape(1, NCORES * TLOC, D)
